# version 16
# baseline (speedup 1.0000x reference)
import numpy as np
import concourse.bass as bass
import concourse.mybir as mybir
from concourse.bass_utils import run_bass_kernel_spmd

F32 = mybir.dt.float32
BF16 = mybir.dt.bfloat16
AF = mybir.ActivationFunctionType
ALU = mybir.AluOpType

NCORES = 8
D = 1024
NDC = D // 128
FH = 16384
NCH = FH // 128
TS = 256
TB = 128
G = 8
NG = NCH // G

ENGINES = ("tensor", "vector", "scalar", "sync", "gpsimd")

_BANDS = [[c for c in range(NCH) if ((c // 2) % 8) // 2 == b] for b in range(4)]
FFN_ORDER = [c for group in zip(*_BANDS) for c in group]
assert sorted(FFN_ORDER) == list(range(NCH))


class Sched:

    def __init__(self):
        self.ops = {e: [] for e in ENGINES}
        self.semv = {}
        self.lastw = {}
        self.readers = {}
        self.ewaits = {e: {} for e in ENGINES}

    def emit(self, eng, fn, reads=(), writes=(), sem=None, inc=None,
             force_token_val=None):
        if sem is None:
            sem, inc = "E_" + eng, 1
        waits = {}

        def addw(toks):
            if not toks:
                return
            for s, v in toks.items():
                if eng == "tensor" and s == "E_tensor":
                    continue
                if self.ewaits[eng].get(s, 0) >= v:
                    continue
                waits[s] = max(waits.get(s, 0), v)

        for b in reads:
            addw(self.lastw.get(b))
        for b in writes:
            addw(self.lastw.get(b))
            addw(self.readers.get(b))
        for s, v in waits.items():
            self.ewaits[eng][s] = v
        self.semv[sem] = self.semv.get(sem, 0) + inc
        tok = (sem, self.semv[sem] if force_token_val is None else force_token_val)
        for b in writes:
            self.lastw[b] = {tok[0]: tok[1]}
            self.readers[b] = {}
        for b in reads:
            if b not in writes:
                r = self.readers.setdefault(b, {})
                r[tok[0]] = max(r.get(tok[0], 0), tok[1])
        self.ops[eng].append((sorted(waits.items()), fn, sem, inc))
        return tok

    def raw_wait(self, eng, sem, val):
        if self.ewaits[eng].get(sem, 0) >= val:
            return
        self.ewaits[eng][sem] = val
        self.ops[eng].append(((("" + sem, val),), None, None, 0))


def build_nc(nseq, tseq):
    rows = nseq * tseq
    nst = rows // TS
    per_seq = tseq // TS
    nc = bass.Bass("TRN2", target_bir_lowering=False)

    x_in = nc.dram_tensor("x_in", [rows, D], F32, kind="ExternalInput").ap()
    w_inT = nc.dram_tensor("w_inT", [NDC, 128, D], F32, kind="ExternalInput").ap()
    w_outT = nc.dram_tensor("w_outT", [NDC, 128, D], F32, kind="ExternalInput").ap()
    consts = nc.dram_tensor("consts", [128, 32], F32, kind="ExternalInput").ap()
    b_out_h = nc.dram_tensor("b_out_h", [1, D], F32, kind="ExternalInput").ap()
    ones_h = nc.dram_tensor("ones_h", [1, 256], F32, kind="ExternalInput").ap()
    b_row_h = nc.dram_tensor("b_row_h", [1, D], F32, kind="ExternalInput").ap()
    bin_row_h = nc.dram_tensor("bin_row_h", [1, D], F32, kind="ExternalInput").ap()
    eye_h = nc.dram_tensor("eye_h", [128, 128], F32, kind="ExternalInput").ap()
    maker_h = nc.dram_tensor("maker_h", [16, 8192], F32, kind="ExternalInput").ap()
    up_h = nc.dram_tensor("up_h", [16, FH], F32, kind="ExternalInput").ap()
    w_arr = nc.dram_tensor("w_arr", [NCH, 128, D], F32, kind="ExternalInput").ap()
    y_out = nc.dram_tensor("y_out", [rows, D], F32, kind="ExternalOutput").ap()
    w_bf = nc.dram_tensor("w_bf", [NCH, 128, D], BF16).ap()

    from contextlib import ExitStack
    ctx = ExitStack()

    def sb(name, shape, dt):
        return ctx.enter_context(nc.sbuf_tensor(name, shape, dt))

    def ps(name, shape):
        return ctx.enter_context(nc.psum_tensor(name, shape, F32))

    W_IN = sb("W_IN", [128, NDC * D], BF16)
    W_OUT = sb("W_OUT", [128, NDC * D], BF16)
    M2Z = sb("M2Z", [128, 4096], BF16)
    EYE = sb("EYE", [128, 128], BF16)
    EYEF = sb("EYEF", [128, 128], F32)
    ONES = sb("ONES", [1, 256], BF16)
    BROW = sb("BROW", [1, D], BF16)
    BINROW = sb("BINROW", [1, D], BF16)
    BBINROW = sb("BBINROW", [1, D], BF16)
    W_INS = sb("W_INS", [128, NDC * D], BF16)
    BB = sb("BB", [128, D], F32)
    BOUT = sb("BOUT", [1, D], BF16)
    CONSTS = sb("CONSTS", [128, 32], F32)
    AVEC = sb("AVEC", [128, 8], F32)
    BBIN = sb("BBIN", [128, 8], F32)
    ABC = sb("ABC", [128, NDC * TS], F32)
    STATE = sb("STATE", [128, NDC], F32)
    STATS = sb("STATS", [128, 16], F32)
    STG_M = sb("STG_M", [16, 2048], F32)
    STG_U = sb("STG_U", [16, 4096], F32)
    XRES = [sb(f"XRES{i}", [128, 2 * D], F32) for i in range(3)]
    XHA = [sb(f"XHA{i}", [128, D], BF16) for i in range(2)]
    XHF = [sb(f"XHF{i}", [128, D], F32) for i in range(2)]
    XTA = sb("XTA", [128, NDC * TS], BF16)
    XTF = sb("XTF", [128, NDC * TS], BF16)
    YBF = sb("YBF", [128, NDC * TS], BF16)
    WSTR = [sb(f"WSTR{i}", [128, G * D], BF16) for i in range(3)]
    HC = [sb(f"HC{i}", [128, TS], BF16) for i in range(8)]
    DUM = sb("DUM", [128, 1], BF16)

    POUT = [ps(f"POUT{i}", [128, 512]) for i in range(4)]
    PMM = [ps(f"PMM{i}", [128, 512]) for i in range(4)]

    sem_names = ["E_tensor", "E_vector", "E_scalar", "DSET", "DSTG_M", "DSTG_U",
                 "DX0", "DX1", "DX2", "DXO0", "DXO1", "DXO2", "DW0", "DW1", "DW2", "DTR",
                 "DTF0", "DTF1", "DTF2", "DTF3", "DTF4", "DTF5", "DTF6", "DTF7",
                 "DCL0", "DCL1", "DCS0", "DCS1"]
    sems = {n: ctx.enter_context(nc.semaphore(n)) for n in sem_names}

    S = Sched()

    setup_loads = [
        (W_IN[:, :], w_inT.rearrange("a p f -> p a f"), "W_IN"),
        (W_OUT[:, :], w_outT.rearrange("a p f -> p a f"), "W_OUT"),
        (CONSTS[:, :], consts, "CONSTS"),
        (BOUT[:, :], b_out_h, "BOUT"),
        (ONES[:, :], ones_h, "ONES"),
        (BROW[:, :], b_row_h, "BROW"),
        (BINROW[:, :], bin_row_h, "BINROW"),
        (EYE[:, :], eye_h, "EYE"),
        (EYEF[:, :], eye_h, "EYEF"),
    ]
    tot = 16 * len(setup_loads)
    for dst, src, buf in setup_loads:
        S.emit("gpsimd", (lambda d, s: (lambda e: e.dma_start(d, s)))(dst, src),
               reads=(), writes=(buf,), sem="DSET", inc=16, force_token_val=tot)

    S.emit("scalar", lambda e: e.activation(AVEC[:, :], CONSTS[:, 0:8], AF.Sigmoid),
           reads=("CONSTS",), writes=("AVEC",))
    S.emit("vector", lambda e: e.tensor_mul(BBIN[:, :], CONSTS[:, 8:16], CONSTS[:, 16:24]),
           reads=("CONSTS",), writes=("BBIN",))
    S.emit("vector", lambda e: e.memset(ABC[:, :], 1.0), writes=("ABC",))
    for ec in range(NDC):
        S.emit("vector",
               (lambda ec: (lambda e: e.tensor_scalar(
                   ABC[:, ec * TS:(ec + 1) * TS], ABC[:, ec * TS:(ec + 1) * TS],
                   AVEC[:, ec:ec + 1], None, ALU.mult)))(ec),
               reads=("ABC", "AVEC"), writes=("ABC",))
    S.emit("vector", lambda e: e.memset(M2Z[:, :], 0.0), writes=("M2Z",))
    S.emit("vector", lambda e: e.tensor_mul(BBINROW[:, :], BROW[:, :], BINROW[:, :]),
           reads=("BROW", "BINROW"), writes=("BBINROW",))
    for eh in range(2):
        S.emit("tensor",
               (lambda eh: (lambda e: e.matmul(
                   POUT[0][:, :], ONES[0:1, 0:128], BROW[0:1, eh * 512:(eh + 1) * 512],
                   start=True, stop=True)))(eh),
               reads=("ONES", "BROW"), writes=("POUT0",))
        S.emit("vector",
               (lambda eh: (lambda e: e.tensor_copy(
                   BB[:, eh * 512:(eh + 1) * 512], POUT[0][:, :])))(eh),
               reads=("POUT0",), writes=("BB",))
    for dc in range(NDC):
        S.emit("vector",
               (lambda dc: (lambda e: e.tensor_mul(
                   W_INS[:, dc * D:(dc + 1) * D], W_IN[:, dc * D:(dc + 1) * D],
                   BB[:, :])))(dc),
               reads=("W_IN", "BB"), writes=("W_INS",))

    for h in range(4):
        S.emit("gpsimd",
               (lambda h: (lambda e: e.dma_start(
                   STG_M[:, :], maker_h[:, h * 2048:(h + 1) * 2048])))(h),
               reads=(), writes=("STG_M",), sem="DSTG_M", inc=16)
        S.emit("gpsimd",
               (lambda h: (lambda e: e.dma_start(
                   STG_U[:, :], up_h[:, h * 4096:(h + 1) * 4096])))(h),
               reads=(), writes=("STG_U",), sem="DSTG_U", inc=16)
        for r in range(64):
            pi = r % 2
            S.emit("tensor",
                   (lambda h, r, pi: (lambda e: e.matmul(
                       PMM[pi][0:32, 0:64],
                       STG_M[0:16, r * 32:(r + 1) * 32],
                       STG_U[0:16, r * 64:(r + 1) * 64],
                       start=True, stop=True)))(h, r, pi),
                   reads=("STG_M", "STG_U"), writes=(f"PMM{pi}",))
            g = r % 8
            g2, sub = g // 2, g % 2
            half, hp = h // 2, h % 2
            slot = ((r // 8) * 2 + sub) * 2 + half
            rr = g2 * 32
            cc0 = slot * 128 + hp * 64
            S.emit("vector",
                   (lambda pi, rr, cc0: (lambda e: e.tensor_copy(
                       M2Z[rr:rr + 32, cc0:cc0 + 64], PMM[pi][0:32, 0:64])))(pi, rr, cc0),
                   reads=(f"PMM{pi}",), writes=("M2Z",))

    def conv_group(g):
        sl = g % 2
        S.emit("gpsimd",
               (lambda g, sl: (lambda e: e.dma_start(
                   WSTR[sl][:, :],
                   w_arr[g * G:(g + 1) * G].rearrange("c f e -> f c e"))))(g, sl),
               reads=(), writes=(f"WSTR{sl}",), sem=f"DCL{sl}", inc=16)
        S.emit("sync",
               (lambda g, sl: (lambda e: e.dma_start(
                   w_bf[g * G:(g + 1) * G].rearrange("c f e -> f c e"),
                   WSTR[sl][:, :])))(g, sl),
               reads=(f"WSTR{sl}",), writes=(f"WBF{g}",), sem=f"DCS{sl}", inc=16)

    def x_load(st):
        sl = st % 3
        S.emit("gpsimd",
               (lambda st, sl: (lambda e: e.dma_start(
                   XRES[sl][:, :],
                   x_in[st * TS:(st + 1) * TS, :].rearrange("(a p) e -> p a e", p=128))))(st, sl),
               reads=(), writes=(f"XRES{sl}",), sem=f"DX{sl}", inc=16)

    def xbuf_k(sl):
        return f"XRES{sl}"

    def ln_stats(xbuf, tb, XH, hkey, apply_eng="vector"):
        o = tb * D
        S.emit("vector",
               (lambda o, xbuf: (lambda e: e.bn_stats(STATS[:, 0:6], XRES[xbuf][:, o:o + 512])))(o, xbuf),
               reads=(xbuf_k(xbuf),), writes=("STATS",))
        S.emit("vector",
               (lambda o, xbuf: (lambda e: e.bn_stats(STATS[:, 6:12], XRES[xbuf][:, o + 512:o + 1024])))(o, xbuf),
               reads=(xbuf_k(xbuf),), writes=("STATS",))
        ln_stats_tail(xbuf, tb, XH, hkey, apply_eng)

    def ln_stats_tail(xbuf, tb, XH, hkey, apply_eng="vector"):
        o = tb * D
        S.emit("vector", lambda e: e.bn_aggr(STATS[:, 12:14], STATS[:, 0:12]),
               reads=("STATS",), writes=("STATS",))
        S.emit("scalar", lambda e: e.activation(
            STATS[:, 15:16], STATS[:, 13:14], AF.Ln, bias=CONSTS[:, 24:25]),
            reads=("STATS", "CONSTS"), writes=("STATS",))
        S.emit("scalar", lambda e: e.activation(
            STATS[:, 14:15], STATS[:, 15:16], AF.Exp, scale=-0.5),
            reads=("STATS",), writes=("STATS",))
        if apply_eng == "vector":
            S.emit("vector",
                   (lambda o, tb, xbuf, XH: (lambda e: e.tensor_scalar(
                       XH[tb][:, :], XRES[xbuf][:, o:o + D],
                       STATS[:, 12:13], STATS[:, 14:15], ALU.subtract, ALU.mult)))(o, tb, xbuf, XH),
                   reads=(xbuf_k(xbuf), "STATS"), writes=(f"{hkey}{tb}",))
        else:
            S.emit("scalar", lambda e: e.mul(STATS[:, 15:16], STATS[:, 12:13],
                                             STATS[:, 14:15]),
                   reads=("STATS",), writes=("STATS",))
            S.emit("scalar", lambda e: e.mul(STATS[:, 15:16], STATS[:, 15:16], -1.0),
                   reads=("STATS",), writes=("STATS",))
            S.emit("scalar",
                   (lambda o, tb, xbuf, XH: (lambda e: e.activation(
                       XH[tb][:, :], XRES[xbuf][:, o:o + D], AF.Identity,
                       bias=STATS[:, 15:16], scale=STATS[:, 14:15])))(o, tb, xbuf, XH),
                   reads=(xbuf_k(xbuf), "STATS"), writes=(f"{hkey}{tb}",))

    def h_group(ec):
        po = ec % 4
        S.emit("tensor",
               (lambda ec, po: (lambda e: e.matmul(
                   PMM[po][:, 0:TS], BBINROW[0:1, ec * 128:(ec + 1) * 128],
                   ONES[0:1, 0:256], start=True, stop=False)))(ec, po),
               reads=("BBINROW", "ONES"), writes=(f"PMM{po}",))
        for dc in range(NDC):
            S.emit("tensor",
                   (lambda ec, dc, po: (lambda e: e.matmul(
                       PMM[po][:, 0:TS],
                       W_INS[:, dc * D + ec * 128: dc * D + (ec + 1) * 128],
                       XTA[:, dc * TS:(dc + 1) * TS],
                       start=False, stop=(dc == NDC - 1))))(ec, dc, po),
                   reads=("W_INS", f"XTAd{dc}_0", f"XTAd{dc}_1"), writes=(f"PMM{po}",))

    def attn_a(st, seq_pos, nst):
        S.emit("scalar", lambda e: e.activation(DUM[:, :], CONSTS[:, 24:25], AF.Ln),
               reads=("CONSTS",), writes=("DUM",))
        h_group(0)
        for ec in range(NDC):
            po = ec % 4
            if ec + 1 < NDC:
                h_group(ec + 1)
            init = 0.0 if seq_pos == 0 else STATE[:, ec:ec + 1]
            rd = (f"PMM{po}", "ABC") if seq_pos == 0 else (f"PMM{po}", "ABC", f"STATE{ec}")
            S.emit("vector",
                   (lambda ec, po, init: (lambda e: e.tensor_tensor_scan(
                       YBF[:, ec * TS:(ec + 1) * TS], ABC[:, ec * TS:(ec + 1) * TS],
                       PMM[po][:, 0:TS], init, ALU.mult, ALU.add)))(ec, po, init),
                   reads=rd, writes=(f"YBF{ec}",))
        if seq_pos != per_seq - 1:
            ybf3 = YBF[:, :].rearrange("p (n t) -> p n t", t=TS)
            S.emit("vector",
                   (lambda ybf3: (lambda e: e.tensor_copy(
                       STATE[:, 0:NDC], ybf3[:, :, TS - 1:TS])))(ybf3),
                   reads=tuple(f"YBF{ec}" for ec in range(NDC)),
                   writes=tuple(f"STATE{ec}" for ec in range(NDC)))
        if st + 1 < nst:
            ln_stats((st + 1) % 3, 0, XHA, "XHA")
            ln_stats((st + 1) % 3, 1, XHA, "XHA")

    def attn_b(xbuf, tr_next):
        for tb in range(2):
            for eh in range(2):
                S.emit("tensor",
                       (lambda tb, eh: (lambda e: e.matmul(
                           POUT[2 * tb + eh][:, :], ONES[0:1, 0:128],
                           BOUT[0:1, eh * 512:(eh + 1) * 512],
                           start=True, stop=False)))(tb, eh),
                       reads=("ONES", "BOUT"), writes=(f"POUT{2 * tb + eh}",))
            for ec in range(NDC):
                for eh in range(2):
                    S.emit("tensor",
                           (lambda ec, tb, eh: (lambda e: e.matmul(
                               POUT[2 * tb + eh][:, :],
                               YBF[:, ec * TS + tb * TB: ec * TS + (tb + 1) * TB],
                               W_OUT[:, ec * D + eh * 512: ec * D + (eh + 1) * 512],
                               start=False, stop=(ec == NDC - 1))))(ec, tb, eh),
                           reads=(f"YBF{ec}", "W_OUT"), writes=(f"POUT{2 * tb + eh}",))
            for eh in range(2):
                S.emit("vector",
                       (lambda tb, eh, xbuf: (lambda e: e.tensor_add(
                           XRES[xbuf][:, tb * D + eh * 512: tb * D + (eh + 1) * 512],
                           XRES[xbuf][:, tb * D + eh * 512: tb * D + (eh + 1) * 512],
                           POUT[2 * tb + eh][:, :])))(tb, eh, xbuf),
                       reads=(xbuf_k(xbuf), f"POUT{2 * tb + eh}"), writes=(xbuf_k(xbuf),))
                S.emit("vector",
                       (lambda tb, eh, xbuf: (lambda e: e.bn_stats(
                           STATS[:, 6 * eh:6 * eh + 6],
                           XRES[xbuf][:, tb * D + eh * 512: tb * D + (eh + 1) * 512])))(tb, eh, xbuf),
                       reads=(xbuf_k(xbuf),), writes=("STATS",))
            ln_stats_tail(xbuf, tb, XHF, "XHF", "scalar")
        gelu_preload()
        pe_trF()
        if tr_next:
            dma_tr_next()

    def gelu_preload():
        S.emit("scalar", lambda e: e.activation(DUM[:, :], CONSTS[:, 24:25], AF.Gelu),
               reads=("CONSTS",), writes=("DUM",))

    def dma_tr_next():
        tot = S.semv.get("DTR", 0) + 16 * 2 * NDC
        for tb in range(2):
            for dc in range(NDC):
                S.emit("sync",
                       (lambda tb, dc: (lambda e: e.dma_start_transpose(
                           XTA[:, dc * TS + tb * TB: dc * TS + (tb + 1) * TB],
                           XHA[tb][:, dc * 128:(dc + 1) * 128])))(tb, dc),
                       reads=(f"XHA{tb}",), writes=(f"XTAd{dc}_{tb}",),
                       sem="DTR", inc=16, force_token_val=tot)

    def pe_trF():
        for k in range(2 * NDC):
            tb, dc = k // NDC, k % NDC
            j, sl = k % 4, (k // 4) % 2
            dst = PMM[j][:, TS + sl * 128: TS + (sl + 1) * 128]
            S.emit("tensor",
                   (lambda tb, dc, dst: (lambda e: e.transpose(
                       dst, XHF[tb][:, dc * 128:(dc + 1) * 128], EYEF[:, :])))(tb, dc, dst),
                   reads=(f"XHF{tb}", "EYEF"), writes=(f"PMM{j}",))
            S.emit("scalar",
                   (lambda tb, dc, dst: (lambda e: e.copy(
                       XTF[:, dc * TS + tb * TB: dc * TS + (tb + 1) * TB], dst)))(tb, dc, dst),
                   reads=(f"PMM{j}",), writes=(f"XTFd{dc}_{tb}",))

    def w_group_load(st, g):
        sl = g % 3
        S.emit("sync",
               (lambda g, sl: (lambda e: e.dma_start(
                   WSTR[sl][:, :],
                   w_bf[g * G:(g + 1) * G].rearrange("c f e -> f c e"))))(g, sl),
               reads=(f"WBF{g}",), writes=(f"WSTR{sl}",), sem=f"DW{sl}", inc=16)

    def ffn(st, xbuf):
        nsl = 2 if st == 0 else 3
        pref = 2 if st == 0 else 3

        def cproj(i):
            c = FFN_ORDER[i]
            g, cc, po = i // G, i % G, i % 8
            sl = g % nsl
            for tb in range(2):
                for eh in range(2):
                    S.emit("tensor",
                           (lambda i, cc, sl, po, tb, eh: (lambda e: e.matmul(
                               POUT[2 * tb + eh][:, :],
                               HC[po][:, tb * TB:(tb + 1) * TB],
                               WSTR[sl][:, cc * D + eh * 512: cc * D + (eh + 1) * 512],
                               start=(i == 0), stop=(i == NCH - 1))))(i, cc, sl, po, tb, eh),
                           reads=(f"HC{po}", f"WSTR{sl}"), writes=(f"POUT{2 * tb + eh}",))
            if cc == G - 1 and g + pref < NG:
                if st == 0:
                    conv_group(g + pref)
                else:
                    w_group_load(st, g + pref)

        def zmm(i):
            c = FFN_ORDER[i]
            r = c // 2
            g2 = (r % 8) // 2
            slot = ((r // 8) * 2 + ((r % 8) % 2)) * 2 + (c % 2)
            po = i % 4
            pap = PMM[po][:, 0:TS]
            pkey = f"PMM{po}"
            S.emit("tensor",
                   (lambda c, g2, slot, pap: (lambda e: e.matmul(
                       pap,
                       M2Z[g2 * 32:(g2 + 1) * 32, slot * 128:(slot + 1) * 128],
                       XTF[g2 * 32:(g2 + 1) * 32, (c // 16) * TS:((c // 16) + 1) * TS],
                       start=True, stop=True,
                       tile_position=(g2 * 32, 0))))(c, g2, slot, pap),
                   reads=("M2Z", f"XTFd{c // 16}_0", f"XTFd{c // 16}_1"),
                   writes=(pkey,))
            S.emit("scalar",
                   (lambda pap, hpo: (lambda e: e.activation(
                       HC[hpo][:, :], pap, AF.Gelu)))(pap, i % 8),
                   reads=(pkey,), writes=(f"HC{i % 8}",))

        done = 0
        for k in range(NCH // 4):
            for j in range(4):
                zmm(4 * k + j)
            while done < 4 * k:
                cproj(done)
                done += 1
        while done < NCH:
            cproj(done)
            done += 1
        for tb in range(2):
            for eh in range(2):
                S.emit("vector",
                       (lambda tb, eh, xbuf: (lambda e: e.tensor_add(
                           XRES[xbuf][:, tb * D + eh * 512: tb * D + (eh + 1) * 512],
                           XRES[xbuf][:, tb * D + eh * 512: tb * D + (eh + 1) * 512],
                           POUT[2 * tb + eh][:, :])))(tb, eh, xbuf),
                       reads=(xbuf_k(xbuf), f"POUT{2 * tb + eh}"), writes=(xbuf_k(xbuf),))

    def x_store(st):
        sl = st % 3
        S.emit("gpsimd",
               (lambda st, sl: (lambda e: e.dma_start(
                   y_out[st * TS:(st + 1) * TS, :].rearrange("(a p) e -> p a e", p=128),
                   XRES[sl][:, :])))(st, sl),
               reads=(f"XRES{sl}",), writes=(), sem=f"DXO{sl}", inc=16)

    x_load(0)
    ln_stats(0, 0, XHA, "XHA")
    ln_stats(0, 1, XHA, "XHA")
    conv_group(0)
    conv_group(1)
    x_load(1)
    for st in range(nst):
        xbuf = st % 3
        seq_pos = st % per_seq
        if st > 0:
            w_group_load(st, 0)
            w_group_load(st, 1)
            w_group_load(st, 2)
        if st == 0:
            dma_tr_next()
        attn_a(st, seq_pos, nst)
        attn_b(xbuf, st + 1 < nst)
        ffn(st, xbuf)
        if st + 2 < nst:
            x_load(st + 2)
        x_store(st)

    S.raw_wait("gpsimd", "DXO0", S.semv.get("DXO0", 0))
    S.raw_wait("gpsimd", "DXO1", S.semv.get("DXO1", 0))
    S.raw_wait("gpsimd", "DXO2", S.semv.get("DXO2", 0))

    def replay(name):
        def f(eng):
            for waits, fn, sem, inc in S.ops[name]:
                for s, v in waits:
                    eng.wait_ge(sems[s], v)
                if fn is not None:
                    fn(eng).then_inc(sems[sem], inc)
        return f

    with nc.Block() as block:
        block.sync(replay("sync"))
        block.gpsimd(replay("gpsimd"))
        block.tensor(replay("tensor"))
        block.vector(replay("vector"))
        block.scalar(replay("scalar"))

    ctx.close()
    return nc


def host_stage(L0_attn_w_in, L0_attn_b_in, L0_attn_a, L0_attn_b, L0_attn_w_out,
               L0_attn_b_out, L0_ffn_rail_maker, L0_ffn_rail_up_proj, L0_ffn_w_cproj):
    f32 = np.float32
    w_inT = np.ascontiguousarray(np.asarray(L0_attn_w_in, f32).T).reshape(NDC, 128, D)
    w_outT = np.ascontiguousarray(np.asarray(L0_attn_w_out, f32).T).reshape(NDC, 128, D)
    consts = np.zeros((128, 32), f32)
    consts[:, 0:8] = np.asarray(L0_attn_a, f32).reshape(8, 128).T
    consts[:, 8:16] = np.asarray(L0_attn_b, f32).reshape(8, 128).T
    consts[:, 16:24] = np.asarray(L0_attn_b_in, f32).reshape(8, 128).T
    consts[:, 24] = 1e-5
    b_out_h = np.ascontiguousarray(np.asarray(L0_attn_b_out, f32)[None, :])
    ones_h = np.ones((1, 256), f32)
    b_row_h = np.ascontiguousarray(np.asarray(L0_attn_b, f32)[None, :])
    bin_row_h = np.ascontiguousarray(np.asarray(L0_attn_b_in, f32)[None, :])
    eye_h = np.eye(128, dtype=f32)
    mk = np.asarray(L0_ffn_rail_maker, f32)
    maker_h = np.zeros((16, 8192), f32)
    for h in range(4):
        for r in range(64):
            sub = (r % 8) % 2
            c0 = (h * 64 + r) * 32 + sub * 16
            maker_h[:, c0:c0 + 16] = mk[h, r].T
    maker_h = np.ascontiguousarray(maker_h)
    up_h = np.ascontiguousarray(
        np.asarray(L0_ffn_rail_up_proj, f32).transpose(3, 0, 1, 2).reshape(16, FH))
    wt = np.ascontiguousarray(np.asarray(L0_ffn_w_cproj, f32).T)
    wt = wt.reshape(2, 2, 64, 64, D)
    wt = wt.transpose(2, 0, 1, 3, 4)
    w_arr = np.ascontiguousarray(wt.reshape(NCH, 128, D)[FFN_ORDER])
    return dict(w_inT=w_inT, w_outT=w_outT, consts=consts, b_out_h=b_out_h,
                ones_h=ones_h, b_row_h=b_row_h, bin_row_h=bin_row_h,
                eye_h=eye_h, maker_h=maker_h, up_h=up_h, w_arr=w_arr)


def kernel(L0_attn_x, L0_attn_w_in, L0_attn_b_in, L0_attn_a, L0_attn_b,
           L0_attn_w_out, L0_attn_b_out, L0_ffn_rail_maker, L0_ffn_rail_up_proj,
           L0_ffn_w_cproj):
    x = np.ascontiguousarray(np.asarray(L0_attn_x, np.float32))
    B, T, _ = x.shape
    nseq = B // NCORES
    weights = host_stage(L0_attn_w_in, L0_attn_b_in, L0_attn_a, L0_attn_b,
                         L0_attn_w_out, L0_attn_b_out, L0_ffn_rail_maker,
                         L0_ffn_rail_up_proj, L0_ffn_w_cproj)
    nc = build_nc(nseq, T)
    in_maps = []
    for core in range(NCORES):
        m = dict(weights)
        m["x_in"] = np.ascontiguousarray(
            x[core * nseq:(core + 1) * nseq].reshape(nseq * T, D))
        in_maps.append(m)
    res = run_bass_kernel_spmd(nc, in_maps, core_ids=list(range(NCORES)))
    outs = [np.asarray(r["y_out"]).reshape(nseq, T, D) for r in res.results]
    return np.concatenate(outs, axis=0)


# version 17
# speedup vs baseline: 1.0070x; 1.0070x over previous
import numpy as np
import concourse.bass as bass
import concourse.mybir as mybir
from concourse.bass_utils import run_bass_kernel_spmd

F32 = mybir.dt.float32
BF16 = mybir.dt.bfloat16
AF = mybir.ActivationFunctionType
ALU = mybir.AluOpType

NCORES = 8
D = 1024
NDC = D // 128
FH = 16384
NCH = FH // 128
TS = 256
TB = 128
G = 8
NG = NCH // G

ENGINES = ("tensor", "vector", "scalar", "sync", "gpsimd")

_BANDS = [[c for c in range(NCH) if ((c // 2) % 8) // 2 == b] for b in range(4)]
FFN_ORDER = [c for group in zip(*_BANDS) for c in group]
assert sorted(FFN_ORDER) == list(range(NCH))


class Sched:

    def __init__(self):
        self.ops = {e: [] for e in ENGINES}
        self.semv = {}
        self.lastw = {}
        self.readers = {}
        self.ewaits = {e: {} for e in ENGINES}

    def emit(self, eng, fn, reads=(), writes=(), sem=None, inc=None,
             force_token_val=None):
        if sem is None:
            sem, inc = "E_" + eng, 1
        waits = {}

        def addw(toks):
            if not toks:
                return
            for s, v in toks.items():
                if eng == "tensor" and s == "E_tensor":
                    continue
                if self.ewaits[eng].get(s, 0) >= v:
                    continue
                waits[s] = max(waits.get(s, 0), v)

        for b in reads:
            addw(self.lastw.get(b))
        for b in writes:
            addw(self.lastw.get(b))
            addw(self.readers.get(b))
        for s, v in waits.items():
            self.ewaits[eng][s] = v
        self.semv[sem] = self.semv.get(sem, 0) + inc
        tok = (sem, self.semv[sem] if force_token_val is None else force_token_val)
        for b in writes:
            self.lastw[b] = {tok[0]: tok[1]}
            self.readers[b] = {}
        for b in reads:
            if b not in writes:
                r = self.readers.setdefault(b, {})
                r[tok[0]] = max(r.get(tok[0], 0), tok[1])
        self.ops[eng].append((sorted(waits.items()), fn, sem, inc))
        return tok

    def raw_wait(self, eng, sem, val):
        if self.ewaits[eng].get(sem, 0) >= val:
            return
        self.ewaits[eng][sem] = val
        self.ops[eng].append(((("" + sem, val),), None, None, 0))


def build_nc(nseq, tseq):
    rows = nseq * tseq
    nst = rows // TS
    per_seq = tseq // TS
    nc = bass.Bass("TRN2", target_bir_lowering=False)

    x_in = nc.dram_tensor("x_in", [rows, D], F32, kind="ExternalInput").ap()
    w_inT = nc.dram_tensor("w_inT", [NDC, 128, D], F32, kind="ExternalInput").ap()
    w_outT = nc.dram_tensor("w_outT", [NDC, 128, D], F32, kind="ExternalInput").ap()
    consts = nc.dram_tensor("consts", [128, 32], F32, kind="ExternalInput").ap()
    b_out_h = nc.dram_tensor("b_out_h", [1, D], F32, kind="ExternalInput").ap()
    ones_h = nc.dram_tensor("ones_h", [1, 256], F32, kind="ExternalInput").ap()
    b_row_h = nc.dram_tensor("b_row_h", [1, D], F32, kind="ExternalInput").ap()
    bin_row_h = nc.dram_tensor("bin_row_h", [1, D], F32, kind="ExternalInput").ap()
    eye_h = nc.dram_tensor("eye_h", [128, 128], F32, kind="ExternalInput").ap()
    maker_h = nc.dram_tensor("maker_h", [16, 8192], F32, kind="ExternalInput").ap()
    up_h = nc.dram_tensor("up_h", [16, FH], F32, kind="ExternalInput").ap()
    w_arr = nc.dram_tensor("w_arr", [NCH, 128, D], F32, kind="ExternalInput").ap()
    y_out = nc.dram_tensor("y_out", [rows, D], F32, kind="ExternalOutput").ap()
    w_bf = nc.dram_tensor("w_bf", [NCH, 128, D], BF16).ap()

    from contextlib import ExitStack
    ctx = ExitStack()

    def sb(name, shape, dt):
        return ctx.enter_context(nc.sbuf_tensor(name, shape, dt))

    def ps(name, shape):
        return ctx.enter_context(nc.psum_tensor(name, shape, F32))

    W_IN = sb("W_IN", [128, NDC * D], BF16)
    W_OUT = sb("W_OUT", [128, NDC * D], BF16)
    M2Z = sb("M2Z", [128, 4096], BF16)
    EYE = sb("EYE", [128, 128], BF16)
    EYEF = sb("EYEF", [128, 128], F32)
    ONES = sb("ONES", [1, 256], BF16)
    BROW = sb("BROW", [1, D], BF16)
    BINROW = sb("BINROW", [1, D], BF16)
    BBINROW = sb("BBINROW", [1, D], BF16)
    W_INS = sb("W_INS", [128, NDC * D], BF16)
    BB = sb("BB", [128, D], F32)
    BOUT = sb("BOUT", [1, D], BF16)
    CONSTS = sb("CONSTS", [128, 32], F32)
    AVEC = sb("AVEC", [128, 8], F32)
    BBIN = sb("BBIN", [128, 8], F32)
    ABC = sb("ABC", [128, NDC * TS], F32)
    STATE = sb("STATE", [128, NDC], F32)
    STATS = sb("STATS", [128, 16], F32)
    STG_M = sb("STG_M", [16, 2048], F32)
    STG_U = sb("STG_U", [16, 4096], F32)
    XRES = [sb(f"XRES{i}", [128, 2 * D], F32) for i in range(3)]
    XHA = [sb(f"XHA{i}", [128, D], BF16) for i in range(2)]
    XHF = [sb(f"XHF{i}", [128, D], F32) for i in range(2)]
    XTA = sb("XTA", [128, NDC * TS], BF16)
    XTF = sb("XTF", [128, NDC * TS], BF16)
    YBF = sb("YBF", [128, NDC * TS], BF16)
    WSTR = [sb(f"WSTR{i}", [128, G * D], BF16) for i in range(3)]
    HC = [sb(f"HC{i}", [128, TS], BF16) for i in range(8)]
    DUM = sb("DUM", [128, 1], BF16)

    POUT = [ps(f"POUT{i}", [128, 512]) for i in range(4)]
    PMM = [ps(f"PMM{i}", [128, 512]) for i in range(4)]

    sem_names = ["E_tensor", "E_vector", "E_scalar", "DSET", "DSTG_M", "DSTG_U",
                 "DX0", "DX1", "DX2", "DXO0", "DXO1", "DXO2", "DW0", "DW1", "DW2", "DTR",
                 "DTF0", "DTF1", "DTF2", "DTF3", "DTF4", "DTF5", "DTF6", "DTF7",
                 "DCL0", "DCL1", "DCS0", "DCS1"]
    sems = {n: ctx.enter_context(nc.semaphore(n)) for n in sem_names}

    S = Sched()

    setup_loads = [
        (W_IN[:, :], w_inT.rearrange("a p f -> p a f"), "W_IN"),
        (W_OUT[:, :], w_outT.rearrange("a p f -> p a f"), "W_OUT"),
        (CONSTS[:, :], consts, "CONSTS"),
        (BOUT[:, :], b_out_h, "BOUT"),
        (ONES[:, :], ones_h, "ONES"),
        (BROW[:, :], b_row_h, "BROW"),
        (BINROW[:, :], bin_row_h, "BINROW"),
        (EYE[:, :], eye_h, "EYE"),
        (EYEF[:, :], eye_h, "EYEF"),
    ]
    tot = 16 * len(setup_loads)
    for dst, src, buf in setup_loads:
        S.emit("gpsimd", (lambda d, s: (lambda e: e.dma_start(d, s)))(dst, src),
               reads=(), writes=(buf,), sem="DSET", inc=16, force_token_val=tot)

    S.emit("scalar", lambda e: e.activation(AVEC[:, :], CONSTS[:, 0:8], AF.Sigmoid),
           reads=("CONSTS",), writes=("AVEC",))
    S.emit("vector", lambda e: e.tensor_mul(BBIN[:, :], CONSTS[:, 8:16], CONSTS[:, 16:24]),
           reads=("CONSTS",), writes=("BBIN",))
    S.emit("vector", lambda e: e.memset(ABC[:, :], 1.0), writes=("ABC",))
    for ec in range(NDC):
        S.emit("vector",
               (lambda ec: (lambda e: e.tensor_scalar(
                   ABC[:, ec * TS:(ec + 1) * TS], ABC[:, ec * TS:(ec + 1) * TS],
                   AVEC[:, ec:ec + 1], None, ALU.mult)))(ec),
               reads=("ABC", "AVEC"), writes=("ABC",))
    S.emit("vector", lambda e: e.memset(M2Z[:, :], 0.0), writes=("M2Z",))
    S.emit("vector", lambda e: e.tensor_mul(BBINROW[:, :], BROW[:, :], BINROW[:, :]),
           reads=("BROW", "BINROW"), writes=("BBINROW",))
    for eh in range(2):
        S.emit("tensor",
               (lambda eh: (lambda e: e.matmul(
                   POUT[0][:, :], ONES[0:1, 0:128], BROW[0:1, eh * 512:(eh + 1) * 512],
                   start=True, stop=True)))(eh),
               reads=("ONES", "BROW"), writes=("POUT0",))
        S.emit("vector",
               (lambda eh: (lambda e: e.tensor_copy(
                   BB[:, eh * 512:(eh + 1) * 512], POUT[0][:, :])))(eh),
               reads=("POUT0",), writes=("BB",))
    for dc in range(NDC):
        S.emit("vector",
               (lambda dc: (lambda e: e.tensor_mul(
                   W_INS[:, dc * D:(dc + 1) * D], W_IN[:, dc * D:(dc + 1) * D],
                   BB[:, :])))(dc),
               reads=("W_IN", "BB"), writes=("W_INS",))

    for h in range(4):
        S.emit("gpsimd",
               (lambda h: (lambda e: e.dma_start(
                   STG_M[:, :], maker_h[:, h * 2048:(h + 1) * 2048])))(h),
               reads=(), writes=("STG_M",), sem="DSTG_M", inc=16)
        S.emit("gpsimd",
               (lambda h: (lambda e: e.dma_start(
                   STG_U[:, :], up_h[:, h * 4096:(h + 1) * 4096])))(h),
               reads=(), writes=("STG_U",), sem="DSTG_U", inc=16)
        for r in range(64):
            pi = r % 2
            S.emit("tensor",
                   (lambda h, r, pi: (lambda e: e.matmul(
                       PMM[pi][0:32, 0:64],
                       STG_M[0:16, r * 32:(r + 1) * 32],
                       STG_U[0:16, r * 64:(r + 1) * 64],
                       start=True, stop=True)))(h, r, pi),
                   reads=("STG_M", "STG_U"), writes=(f"PMM{pi}",))
            g = r % 8
            g2, sub = g // 2, g % 2
            half, hp = h // 2, h % 2
            slot = ((r // 8) * 2 + sub) * 2 + half
            rr = g2 * 32
            cc0 = slot * 128 + hp * 64
            S.emit("vector",
                   (lambda pi, rr, cc0: (lambda e: e.tensor_copy(
                       M2Z[rr:rr + 32, cc0:cc0 + 64], PMM[pi][0:32, 0:64])))(pi, rr, cc0),
                   reads=(f"PMM{pi}",), writes=("M2Z",))

    def conv_group(g):
        sl = g % 2
        S.emit("gpsimd",
               (lambda g, sl: (lambda e: e.dma_start(
                   WSTR[sl][:, :],
                   w_arr[g * G:(g + 1) * G].rearrange("c f e -> f c e"))))(g, sl),
               reads=(), writes=(f"WSTR{sl}",), sem=f"DCL{sl}", inc=16)
        S.emit("sync",
               (lambda g, sl: (lambda e: e.dma_start(
                   w_bf[g * G:(g + 1) * G].rearrange("c f e -> f c e"),
                   WSTR[sl][:, :])))(g, sl),
               reads=(f"WSTR{sl}",), writes=(f"WBF{g}",), sem=f"DCS{sl}", inc=16)

    def x_load(st):
        sl = st % 3
        S.emit("gpsimd",
               (lambda st, sl: (lambda e: e.dma_start(
                   XRES[sl][:, :],
                   x_in[st * TS:(st + 1) * TS, :].rearrange("(a p) e -> p a e", p=128))))(st, sl),
               reads=(), writes=(f"XRES{sl}",), sem=f"DX{sl}", inc=16)

    def xbuf_k(sl):
        return f"XRES{sl}"

    def ln_stats(xbuf, tb, XH, hkey, apply_eng="vector"):
        o = tb * D
        S.emit("vector",
               (lambda o, xbuf: (lambda e: e.bn_stats(STATS[:, 0:6], XRES[xbuf][:, o:o + 512])))(o, xbuf),
               reads=(xbuf_k(xbuf),), writes=("STATS",))
        S.emit("vector",
               (lambda o, xbuf: (lambda e: e.bn_stats(STATS[:, 6:12], XRES[xbuf][:, o + 512:o + 1024])))(o, xbuf),
               reads=(xbuf_k(xbuf),), writes=("STATS",))
        ln_stats_tail(xbuf, tb, XH, hkey, apply_eng)

    def ln_stats_tail(xbuf, tb, XH, hkey, apply_eng="vector"):
        o = tb * D
        S.emit("vector", lambda e: e.bn_aggr(STATS[:, 12:14], STATS[:, 0:12]),
               reads=("STATS",), writes=("STATS",))
        S.emit("scalar", lambda e: e.activation(
            STATS[:, 15:16], STATS[:, 13:14], AF.Ln, bias=CONSTS[:, 24:25]),
            reads=("STATS", "CONSTS"), writes=("STATS",))
        S.emit("scalar", lambda e: e.activation(
            STATS[:, 14:15], STATS[:, 15:16], AF.Exp, scale=-0.5),
            reads=("STATS",), writes=("STATS",))
        if apply_eng == "vector":
            S.emit("vector",
                   (lambda o, tb, xbuf, XH: (lambda e: e.tensor_scalar(
                       XH[tb][:, :], XRES[xbuf][:, o:o + D],
                       STATS[:, 12:13], STATS[:, 14:15], ALU.subtract, ALU.mult)))(o, tb, xbuf, XH),
                   reads=(xbuf_k(xbuf), "STATS"), writes=(f"{hkey}{tb}",))
        else:
            S.emit("scalar", lambda e: e.mul(STATS[:, 15:16], STATS[:, 12:13],
                                             STATS[:, 14:15]),
                   reads=("STATS",), writes=("STATS",))
            S.emit("scalar", lambda e: e.mul(STATS[:, 15:16], STATS[:, 15:16], -1.0),
                   reads=("STATS",), writes=("STATS",))
            S.emit("scalar",
                   (lambda o, tb, xbuf, XH: (lambda e: e.activation(
                       XH[tb][:, :], XRES[xbuf][:, o:o + D], AF.Identity,
                       bias=STATS[:, 15:16], scale=STATS[:, 14:15])))(o, tb, xbuf, XH),
                   reads=(xbuf_k(xbuf), "STATS"), writes=(f"{hkey}{tb}",))

    def h_group(ec):
        po = ec % 4
        S.emit("tensor",
               (lambda ec, po: (lambda e: e.matmul(
                   PMM[po][:, 0:TS], BBINROW[0:1, ec * 128:(ec + 1) * 128],
                   ONES[0:1, 0:256], start=True, stop=False)))(ec, po),
               reads=("BBINROW", "ONES"), writes=(f"PMM{po}",))
        for dc in range(NDC):
            S.emit("tensor",
                   (lambda ec, dc, po: (lambda e: e.matmul(
                       PMM[po][:, 0:TS],
                       W_INS[:, dc * D + ec * 128: dc * D + (ec + 1) * 128],
                       XTA[:, dc * TS:(dc + 1) * TS],
                       start=False, stop=(dc == NDC - 1))))(ec, dc, po),
                   reads=("W_INS", f"XTAd{dc}_0", f"XTAd{dc}_1"), writes=(f"PMM{po}",))

    def attn_a(st, seq_pos, nst):
        S.emit("scalar", lambda e: e.activation(DUM[:, :], CONSTS[:, 24:25], AF.Ln),
               reads=("CONSTS",), writes=("DUM",))
        h_group(0)
        for ec in range(NDC):
            po = ec % 4
            if ec + 1 < NDC:
                h_group(ec + 1)
            init = 0.0 if seq_pos == 0 else STATE[:, ec:ec + 1]
            rd = (f"PMM{po}", "ABC") if seq_pos == 0 else (f"PMM{po}", "ABC", f"STATE{ec}")
            S.emit("vector",
                   (lambda ec, po, init: (lambda e: e.tensor_tensor_scan(
                       YBF[:, ec * TS:(ec + 1) * TS], ABC[:, ec * TS:(ec + 1) * TS],
                       PMM[po][:, 0:TS], init, ALU.mult, ALU.add)))(ec, po, init),
                   reads=rd, writes=(f"YBF{ec}",))
        if seq_pos != per_seq - 1:
            ybf3 = YBF[:, :].rearrange("p (n t) -> p n t", t=TS)
            S.emit("vector",
                   (lambda ybf3: (lambda e: e.tensor_copy(
                       STATE[:, 0:NDC], ybf3[:, :, TS - 1:TS])))(ybf3),
                   reads=tuple(f"YBF{ec}" for ec in range(NDC)),
                   writes=tuple(f"STATE{ec}" for ec in range(NDC)))
        if st + 1 < nst:
            ln_stats((st + 1) % 3, 0, XHA, "XHA")
            ln_stats((st + 1) % 3, 1, XHA, "XHA")

    def attn_b(xbuf, tr_next):
        for tb in range(2):
            for eh in range(2):
                S.emit("tensor",
                       (lambda tb, eh: (lambda e: e.matmul(
                           POUT[2 * tb + eh][:, :], ONES[0:1, 0:128],
                           BOUT[0:1, eh * 512:(eh + 1) * 512],
                           start=True, stop=False)))(tb, eh),
                       reads=("ONES", "BOUT"), writes=(f"POUT{2 * tb + eh}",))
            for ec in range(NDC):
                for eh in range(2):
                    S.emit("tensor",
                           (lambda ec, tb, eh: (lambda e: e.matmul(
                               POUT[2 * tb + eh][:, :],
                               YBF[:, ec * TS + tb * TB: ec * TS + (tb + 1) * TB],
                               W_OUT[:, ec * D + eh * 512: ec * D + (eh + 1) * 512],
                               start=False, stop=(ec == NDC - 1))))(ec, tb, eh),
                           reads=(f"YBF{ec}", "W_OUT"), writes=(f"POUT{2 * tb + eh}",))
            for eh in range(2):
                S.emit("vector",
                       (lambda tb, eh, xbuf: (lambda e: e.tensor_add(
                           XRES[xbuf][:, tb * D + eh * 512: tb * D + (eh + 1) * 512],
                           XRES[xbuf][:, tb * D + eh * 512: tb * D + (eh + 1) * 512],
                           POUT[2 * tb + eh][:, :])))(tb, eh, xbuf),
                       reads=(xbuf_k(xbuf), f"POUT{2 * tb + eh}"), writes=(xbuf_k(xbuf),))
                S.emit("vector",
                       (lambda tb, eh, xbuf: (lambda e: e.bn_stats(
                           STATS[:, 6 * eh:6 * eh + 6],
                           XRES[xbuf][:, tb * D + eh * 512: tb * D + (eh + 1) * 512])))(tb, eh, xbuf),
                       reads=(xbuf_k(xbuf),), writes=("STATS",))
            ln_stats_tail(xbuf, tb, XHF, "XHF", "scalar")
        pe_trF()
        if tr_next:
            dma_tr_next()

    def gelu_preload():
        S.emit("scalar", lambda e: e.activation(DUM[:, :], CONSTS[:, 24:25], AF.Gelu),
               reads=("CONSTS",), writes=("DUM",))

    def dma_tr_next():
        tot = S.semv.get("DTR", 0) + 16 * 2 * NDC
        for tb in range(2):
            for dc in range(NDC):
                S.emit("sync",
                       (lambda tb, dc: (lambda e: e.dma_start_transpose(
                           XTA[:, dc * TS + tb * TB: dc * TS + (tb + 1) * TB],
                           XHA[tb][:, dc * 128:(dc + 1) * 128])))(tb, dc),
                       reads=(f"XHA{tb}",), writes=(f"XTAd{dc}_{tb}",),
                       sem="DTR", inc=16, force_token_val=tot)

    def pe_trF():
        for k in range(2 * NDC):
            tb, dc = k // NDC, k % NDC
            j, sl = k % 4, (k // 4) % 2
            dst = PMM[j][:, TS + sl * 128: TS + (sl + 1) * 128]
            S.emit("tensor",
                   (lambda tb, dc, dst: (lambda e: e.transpose(
                       dst, XHF[tb][:, dc * 128:(dc + 1) * 128], EYEF[:, :])))(tb, dc, dst),
                   reads=(f"XHF{tb}", "EYEF"), writes=(f"PMM{j}",))
            S.emit("scalar",
                   (lambda tb, dc, dst: (lambda e: e.copy(
                       XTF[:, dc * TS + tb * TB: dc * TS + (tb + 1) * TB], dst)))(tb, dc, dst),
                   reads=(f"PMM{j}",), writes=(f"XTFd{dc}_{tb}",))
            if k == NDC:
                gelu_preload()

    def w_group_load(st, g):
        sl = g % 3
        S.emit("sync",
               (lambda g, sl: (lambda e: e.dma_start(
                   WSTR[sl][:, :],
                   w_bf[g * G:(g + 1) * G].rearrange("c f e -> f c e"))))(g, sl),
               reads=(f"WBF{g}",), writes=(f"WSTR{sl}",), sem=f"DW{sl}", inc=16)

    def ffn(st, xbuf):
        nsl = 2 if st == 0 else 3
        pref = 2 if st == 0 else 3

        def cproj(i):
            c = FFN_ORDER[i]
            g, cc, po = i // G, i % G, i % 8
            sl = g % nsl
            for tb in range(2):
                for eh in range(2):
                    S.emit("tensor",
                           (lambda i, cc, sl, po, tb, eh: (lambda e: e.matmul(
                               POUT[2 * tb + eh][:, :],
                               HC[po][:, tb * TB:(tb + 1) * TB],
                               WSTR[sl][:, cc * D + eh * 512: cc * D + (eh + 1) * 512],
                               start=(i == 0), stop=(i == NCH - 1))))(i, cc, sl, po, tb, eh),
                           reads=(f"HC{po}", f"WSTR{sl}"), writes=(f"POUT{2 * tb + eh}",))
            if cc == G - 1 and g + pref < NG:
                if st == 0:
                    conv_group(g + pref)
                else:
                    w_group_load(st, g + pref)

        def zmm(i):
            c = FFN_ORDER[i]
            r = c // 2
            g2 = (r % 8) // 2
            slot = ((r // 8) * 2 + ((r % 8) % 2)) * 2 + (c % 2)
            po = i % 4
            pap = PMM[po][:, 0:TS]
            pkey = f"PMM{po}"
            S.emit("tensor",
                   (lambda c, g2, slot, pap: (lambda e: e.matmul(
                       pap,
                       M2Z[g2 * 32:(g2 + 1) * 32, slot * 128:(slot + 1) * 128],
                       XTF[g2 * 32:(g2 + 1) * 32, (c // 16) * TS:((c // 16) + 1) * TS],
                       start=True, stop=True,
                       tile_position=(g2 * 32, 0))))(c, g2, slot, pap),
                   reads=("M2Z", f"XTFd{c // 16}_0", f"XTFd{c // 16}_1"),
                   writes=(pkey,))
            S.emit("scalar",
                   (lambda pap, hpo: (lambda e: e.activation(
                       HC[hpo][:, :], pap, AF.Gelu)))(pap, i % 8),
                   reads=(pkey,), writes=(f"HC{i % 8}",))

        done = 0
        for k in range(NCH // 4):
            for j in range(4):
                zmm(4 * k + j)
            while done < 4 * k:
                cproj(done)
                done += 1
        while done < NCH:
            cproj(done)
            done += 1
        for tb in range(2):
            for eh in range(2):
                S.emit("vector",
                       (lambda tb, eh, xbuf: (lambda e: e.tensor_add(
                           XRES[xbuf][:, tb * D + eh * 512: tb * D + (eh + 1) * 512],
                           XRES[xbuf][:, tb * D + eh * 512: tb * D + (eh + 1) * 512],
                           POUT[2 * tb + eh][:, :])))(tb, eh, xbuf),
                       reads=(xbuf_k(xbuf), f"POUT{2 * tb + eh}"), writes=(xbuf_k(xbuf),))

    def x_store(st):
        sl = st % 3
        S.emit("gpsimd",
               (lambda st, sl: (lambda e: e.dma_start(
                   y_out[st * TS:(st + 1) * TS, :].rearrange("(a p) e -> p a e", p=128),
                   XRES[sl][:, :])))(st, sl),
               reads=(f"XRES{sl}",), writes=(), sem=f"DXO{sl}", inc=16)

    x_load(0)
    ln_stats(0, 0, XHA, "XHA")
    ln_stats(0, 1, XHA, "XHA")
    conv_group(0)
    conv_group(1)
    x_load(1)
    for st in range(nst):
        xbuf = st % 3
        seq_pos = st % per_seq
        if st > 0:
            w_group_load(st, 0)
            w_group_load(st, 1)
            w_group_load(st, 2)
        if st == 0:
            dma_tr_next()
        attn_a(st, seq_pos, nst)
        attn_b(xbuf, st + 1 < nst)
        ffn(st, xbuf)
        if st + 2 < nst:
            x_load(st + 2)
        x_store(st)

    S.raw_wait("gpsimd", "DXO0", S.semv.get("DXO0", 0))
    S.raw_wait("gpsimd", "DXO1", S.semv.get("DXO1", 0))
    S.raw_wait("gpsimd", "DXO2", S.semv.get("DXO2", 0))

    def replay(name):
        def f(eng):
            for waits, fn, sem, inc in S.ops[name]:
                for s, v in waits:
                    eng.wait_ge(sems[s], v)
                if fn is not None:
                    fn(eng).then_inc(sems[sem], inc)
        return f

    with nc.Block() as block:
        block.sync(replay("sync"))
        block.gpsimd(replay("gpsimd"))
        block.tensor(replay("tensor"))
        block.vector(replay("vector"))
        block.scalar(replay("scalar"))

    ctx.close()
    return nc


def host_stage(L0_attn_w_in, L0_attn_b_in, L0_attn_a, L0_attn_b, L0_attn_w_out,
               L0_attn_b_out, L0_ffn_rail_maker, L0_ffn_rail_up_proj, L0_ffn_w_cproj):
    f32 = np.float32
    w_inT = np.ascontiguousarray(np.asarray(L0_attn_w_in, f32).T).reshape(NDC, 128, D)
    w_outT = np.ascontiguousarray(np.asarray(L0_attn_w_out, f32).T).reshape(NDC, 128, D)
    consts = np.zeros((128, 32), f32)
    consts[:, 0:8] = np.asarray(L0_attn_a, f32).reshape(8, 128).T
    consts[:, 8:16] = np.asarray(L0_attn_b, f32).reshape(8, 128).T
    consts[:, 16:24] = np.asarray(L0_attn_b_in, f32).reshape(8, 128).T
    consts[:, 24] = 1e-5
    b_out_h = np.ascontiguousarray(np.asarray(L0_attn_b_out, f32)[None, :])
    ones_h = np.ones((1, 256), f32)
    b_row_h = np.ascontiguousarray(np.asarray(L0_attn_b, f32)[None, :])
    bin_row_h = np.ascontiguousarray(np.asarray(L0_attn_b_in, f32)[None, :])
    eye_h = np.eye(128, dtype=f32)
    mk = np.asarray(L0_ffn_rail_maker, f32)
    maker_h = np.zeros((16, 8192), f32)
    for h in range(4):
        for r in range(64):
            sub = (r % 8) % 2
            c0 = (h * 64 + r) * 32 + sub * 16
            maker_h[:, c0:c0 + 16] = mk[h, r].T
    maker_h = np.ascontiguousarray(maker_h)
    up_h = np.ascontiguousarray(
        np.asarray(L0_ffn_rail_up_proj, f32).transpose(3, 0, 1, 2).reshape(16, FH))
    wt = np.ascontiguousarray(np.asarray(L0_ffn_w_cproj, f32).T)
    wt = wt.reshape(2, 2, 64, 64, D)
    wt = wt.transpose(2, 0, 1, 3, 4)
    w_arr = np.ascontiguousarray(wt.reshape(NCH, 128, D)[FFN_ORDER])
    return dict(w_inT=w_inT, w_outT=w_outT, consts=consts, b_out_h=b_out_h,
                ones_h=ones_h, b_row_h=b_row_h, bin_row_h=bin_row_h,
                eye_h=eye_h, maker_h=maker_h, up_h=up_h, w_arr=w_arr)


def kernel(L0_attn_x, L0_attn_w_in, L0_attn_b_in, L0_attn_a, L0_attn_b,
           L0_attn_w_out, L0_attn_b_out, L0_ffn_rail_maker, L0_ffn_rail_up_proj,
           L0_ffn_w_cproj):
    x = np.ascontiguousarray(np.asarray(L0_attn_x, np.float32))
    B, T, _ = x.shape
    nseq = B // NCORES
    weights = host_stage(L0_attn_w_in, L0_attn_b_in, L0_attn_a, L0_attn_b,
                         L0_attn_w_out, L0_attn_b_out, L0_ffn_rail_maker,
                         L0_ffn_rail_up_proj, L0_ffn_w_cproj)
    nc = build_nc(nseq, T)
    in_maps = []
    for core in range(NCORES):
        m = dict(weights)
        m["x_in"] = np.ascontiguousarray(
            x[core * nseq:(core + 1) * nseq].reshape(nseq * T, D))
        in_maps.append(m)
    res = run_bass_kernel_spmd(nc, in_maps, core_ids=list(range(NCORES)))
    outs = [np.asarray(r["y_out"]).reshape(nseq, T, D) for r in res.results]
    return np.concatenate(outs, axis=0)


# version 18
# speedup vs baseline: 1.0113x; 1.0043x over previous
import numpy as np
import concourse.bass as bass
import concourse.mybir as mybir
from concourse.bass_utils import run_bass_kernel_spmd

F32 = mybir.dt.float32
BF16 = mybir.dt.bfloat16
AF = mybir.ActivationFunctionType
ALU = mybir.AluOpType

NCORES = 8
D = 1024
NDC = D // 128
FH = 16384
NCH = FH // 128
TS = 256
TB = 128
G = 8
NG = NCH // G

ENGINES = ("tensor", "vector", "scalar", "sync", "gpsimd")

_BANDS = [[c for c in range(NCH) if ((c // 2) % 8) // 2 == b] for b in range(4)]
FFN_ORDER = [c for group in zip(*_BANDS) for c in group]
assert sorted(FFN_ORDER) == list(range(NCH))


class Sched:

    def __init__(self):
        self.ops = {e: [] for e in ENGINES}
        self.semv = {}
        self.lastw = {}
        self.readers = {}
        self.ewaits = {e: {} for e in ENGINES}

    def emit(self, eng, fn, reads=(), writes=(), sem=None, inc=None,
             force_token_val=None):
        if sem is None:
            sem, inc = "E_" + eng, 1
        waits = {}

        def addw(toks):
            if not toks:
                return
            for s, v in toks.items():
                if eng == "tensor" and s == "E_tensor":
                    continue
                if self.ewaits[eng].get(s, 0) >= v:
                    continue
                waits[s] = max(waits.get(s, 0), v)

        for b in reads:
            addw(self.lastw.get(b))
        for b in writes:
            addw(self.lastw.get(b))
            addw(self.readers.get(b))
        for s, v in waits.items():
            self.ewaits[eng][s] = v
        self.semv[sem] = self.semv.get(sem, 0) + inc
        tok = (sem, self.semv[sem] if force_token_val is None else force_token_val)
        for b in writes:
            self.lastw[b] = {tok[0]: tok[1]}
            self.readers[b] = {}
        for b in reads:
            if b not in writes:
                r = self.readers.setdefault(b, {})
                r[tok[0]] = max(r.get(tok[0], 0), tok[1])
        self.ops[eng].append((sorted(waits.items()), fn, sem, inc))
        return tok

    def raw_wait(self, eng, sem, val):
        if self.ewaits[eng].get(sem, 0) >= val:
            return
        self.ewaits[eng][sem] = val
        self.ops[eng].append(((("" + sem, val),), None, None, 0))


def build_nc(nseq, tseq):
    rows = nseq * tseq
    nst = rows // TS
    per_seq = tseq // TS
    nc = bass.Bass("TRN2", target_bir_lowering=False)

    x_in = nc.dram_tensor("x_in", [rows, D], F32, kind="ExternalInput").ap()
    w_inT = nc.dram_tensor("w_inT", [NDC, 128, D], F32, kind="ExternalInput").ap()
    w_outT = nc.dram_tensor("w_outT", [NDC, 128, D], F32, kind="ExternalInput").ap()
    consts = nc.dram_tensor("consts", [128, 32], F32, kind="ExternalInput").ap()
    b_out_h = nc.dram_tensor("b_out_h", [1, D], F32, kind="ExternalInput").ap()
    ones_h = nc.dram_tensor("ones_h", [1, 256], F32, kind="ExternalInput").ap()
    b_row_h = nc.dram_tensor("b_row_h", [1, D], F32, kind="ExternalInput").ap()
    bin_row_h = nc.dram_tensor("bin_row_h", [1, D], F32, kind="ExternalInput").ap()
    eye_h = nc.dram_tensor("eye_h", [128, 128], F32, kind="ExternalInput").ap()
    maker_h = nc.dram_tensor("maker_h", [16, 8192], F32, kind="ExternalInput").ap()
    up_h = nc.dram_tensor("up_h", [16, FH], F32, kind="ExternalInput").ap()
    w_arr = nc.dram_tensor("w_arr", [NCH, 128, D], F32, kind="ExternalInput").ap()
    y_out = nc.dram_tensor("y_out", [rows, D], F32, kind="ExternalOutput").ap()
    w_bf = nc.dram_tensor("w_bf", [NCH, 128, D], BF16).ap()

    from contextlib import ExitStack
    ctx = ExitStack()

    def sb(name, shape, dt):
        return ctx.enter_context(nc.sbuf_tensor(name, shape, dt))

    def ps(name, shape):
        return ctx.enter_context(nc.psum_tensor(name, shape, F32))

    W_OUT = sb("W_OUT", [128, NDC * D], BF16)
    M2Z = sb("M2Z", [128, 4096], BF16)
    EYE = sb("EYE", [128, 128], BF16)
    EYEF = sb("EYEF", [128, 128], F32)
    ONES = sb("ONES", [1, 256], BF16)
    BROW = sb("BROW", [1, D], BF16)
    BINROW = sb("BINROW", [1, D], BF16)
    BBINROW = sb("BBINROW", [1, D], BF16)
    W_INS = sb("W_INS", [128, NDC * D], BF16)
    BB = sb("BB", [128, D], F32)
    BOUT = sb("BOUT", [1, D], BF16)
    CONSTS = sb("CONSTS", [128, 32], F32)
    AVEC = sb("AVEC", [128, 8], F32)
    BBIN = sb("BBIN", [128, 8], F32)
    ABC = sb("ABC", [128, NDC * TS], F32)
    STATE = sb("STATE", [128, NDC], F32)
    STATS = sb("STATS", [128, 16], F32)
    STG_M = sb("STG_M", [16, 2048], F32)
    STG_U = sb("STG_U", [16, 4096], F32)
    XRES = [sb(f"XRES{i}", [128, 2 * D], F32) for i in range(3)]
    XHA = [sb(f"XHA{i}", [128, D], BF16) for i in range(2)]
    XHF = [sb(f"XHF{i}", [128, D], F32) for i in range(2)]
    XTA = sb("XTA", [128, NDC * TS], BF16)
    XTF = sb("XTF", [128, NDC * TS], BF16)
    YBF = sb("YBF", [128, NDC * TS], BF16)
    WSTR = [sb(f"WSTR{i}", [128, G * D], BF16) for i in range(4)]
    HC = [sb(f"HC{i}", [128, TS], BF16) for i in range(8)]
    DUM = sb("DUM", [128, 1], BF16)

    POUT = [ps(f"POUT{i}", [128, 512]) for i in range(4)]
    PMM = [ps(f"PMM{i}", [128, 512]) for i in range(4)]

    sem_names = ["E_tensor", "E_vector", "E_scalar", "DSET", "DSTG_M", "DSTG_U",
                 "DX0", "DX1", "DX2", "DXO0", "DXO1", "DXO2", "DW0", "DW1", "DW2", "DW3", "DTR",
                 "DTF0", "DTF1", "DTF2", "DTF3", "DTF4", "DTF5", "DTF6", "DTF7",
                 "DCL0", "DCL1", "DCS0", "DCS1"]
    sems = {n: ctx.enter_context(nc.semaphore(n)) for n in sem_names}

    S = Sched()

    setup_loads = [
        (WSTR[0][:, :], w_inT.rearrange("a p f -> p a f"), "WSTR0"),
        (W_OUT[:, :], w_outT.rearrange("a p f -> p a f"), "W_OUT"),
        (CONSTS[:, :], consts, "CONSTS"),
        (BOUT[:, :], b_out_h, "BOUT"),
        (ONES[:, :], ones_h, "ONES"),
        (BROW[:, :], b_row_h, "BROW"),
        (BINROW[:, :], bin_row_h, "BINROW"),
        (EYE[:, :], eye_h, "EYE"),
        (EYEF[:, :], eye_h, "EYEF"),
    ]
    tot = 16 * len(setup_loads)
    for dst, src, buf in setup_loads:
        S.emit("gpsimd", (lambda d, s: (lambda e: e.dma_start(d, s)))(dst, src),
               reads=(), writes=(buf,), sem="DSET", inc=16, force_token_val=tot)

    S.emit("scalar", lambda e: e.activation(AVEC[:, :], CONSTS[:, 0:8], AF.Sigmoid),
           reads=("CONSTS",), writes=("AVEC",))
    S.emit("vector", lambda e: e.tensor_mul(BBIN[:, :], CONSTS[:, 8:16], CONSTS[:, 16:24]),
           reads=("CONSTS",), writes=("BBIN",))
    S.emit("vector", lambda e: e.memset(ABC[:, :], 1.0), writes=("ABC",))
    for ec in range(NDC):
        S.emit("vector",
               (lambda ec: (lambda e: e.tensor_scalar(
                   ABC[:, ec * TS:(ec + 1) * TS], ABC[:, ec * TS:(ec + 1) * TS],
                   AVEC[:, ec:ec + 1], None, ALU.mult)))(ec),
               reads=("ABC", "AVEC"), writes=("ABC",))
    S.emit("vector", lambda e: e.memset(M2Z[:, :], 0.0), writes=("M2Z",))
    S.emit("vector", lambda e: e.tensor_mul(BBINROW[:, :], BROW[:, :], BINROW[:, :]),
           reads=("BROW", "BINROW"), writes=("BBINROW",))
    for eh in range(2):
        S.emit("tensor",
               (lambda eh: (lambda e: e.matmul(
                   POUT[0][:, :], ONES[0:1, 0:128], BROW[0:1, eh * 512:(eh + 1) * 512],
                   start=True, stop=True)))(eh),
               reads=("ONES", "BROW"), writes=("POUT0",))
        S.emit("vector",
               (lambda eh: (lambda e: e.tensor_copy(
                   BB[:, eh * 512:(eh + 1) * 512], POUT[0][:, :])))(eh),
               reads=("POUT0",), writes=("BB",))
    for dc in range(NDC):
        S.emit("vector",
               (lambda dc: (lambda e: e.tensor_mul(
                   W_INS[:, dc * D:(dc + 1) * D], WSTR[0][:, dc * D:(dc + 1) * D],
                   BB[:, :])))(dc),
               reads=("WSTR0", "BB"), writes=("W_INS",))

    for h in range(4):
        S.emit("gpsimd",
               (lambda h: (lambda e: e.dma_start(
                   STG_M[:, :], maker_h[:, h * 2048:(h + 1) * 2048])))(h),
               reads=(), writes=("STG_M",), sem="DSTG_M", inc=16)
        S.emit("gpsimd",
               (lambda h: (lambda e: e.dma_start(
                   STG_U[:, :], up_h[:, h * 4096:(h + 1) * 4096])))(h),
               reads=(), writes=("STG_U",), sem="DSTG_U", inc=16)
        for r in range(64):
            pi = r % 2
            S.emit("tensor",
                   (lambda h, r, pi: (lambda e: e.matmul(
                       PMM[pi][0:32, 0:64],
                       STG_M[0:16, r * 32:(r + 1) * 32],
                       STG_U[0:16, r * 64:(r + 1) * 64],
                       start=True, stop=True)))(h, r, pi),
                   reads=("STG_M", "STG_U"), writes=(f"PMM{pi}",))
            g = r % 8
            g2, sub = g // 2, g % 2
            half, hp = h // 2, h % 2
            slot = ((r // 8) * 2 + sub) * 2 + half
            rr = g2 * 32
            cc0 = slot * 128 + hp * 64
            S.emit("vector",
                   (lambda pi, rr, cc0: (lambda e: e.tensor_copy(
                       M2Z[rr:rr + 32, cc0:cc0 + 64], PMM[pi][0:32, 0:64])))(pi, rr, cc0),
                   reads=(f"PMM{pi}",), writes=("M2Z",))

    def conv_group(g):
        sl = g % 2
        S.emit("gpsimd",
               (lambda g, sl: (lambda e: e.dma_start(
                   WSTR[sl][:, :],
                   w_arr[g * G:(g + 1) * G].rearrange("c f e -> f c e"))))(g, sl),
               reads=(), writes=(f"WSTR{sl}",), sem=f"DCL{sl}", inc=16)
        S.emit("sync",
               (lambda g, sl: (lambda e: e.dma_start(
                   w_bf[g * G:(g + 1) * G].rearrange("c f e -> f c e"),
                   WSTR[sl][:, :])))(g, sl),
               reads=(f"WSTR{sl}",), writes=(f"WBF{g}",), sem=f"DCS{sl}", inc=16)

    def x_load(st):
        sl = st % 3
        S.emit("gpsimd",
               (lambda st, sl: (lambda e: e.dma_start(
                   XRES[sl][:, :],
                   x_in[st * TS:(st + 1) * TS, :].rearrange("(a p) e -> p a e", p=128))))(st, sl),
               reads=(), writes=(f"XRES{sl}",), sem=f"DX{sl}", inc=16)

    def xbuf_k(sl):
        return f"XRES{sl}"

    def ln_stats(xbuf, tb, XH, hkey, apply_eng="vector"):
        o = tb * D
        S.emit("vector",
               (lambda o, xbuf: (lambda e: e.bn_stats(STATS[:, 0:6], XRES[xbuf][:, o:o + 512])))(o, xbuf),
               reads=(xbuf_k(xbuf),), writes=("STATS",))
        S.emit("vector",
               (lambda o, xbuf: (lambda e: e.bn_stats(STATS[:, 6:12], XRES[xbuf][:, o + 512:o + 1024])))(o, xbuf),
               reads=(xbuf_k(xbuf),), writes=("STATS",))
        ln_stats_tail(xbuf, tb, XH, hkey, apply_eng)

    def ln_stats_tail(xbuf, tb, XH, hkey, apply_eng="vector"):
        o = tb * D
        S.emit("vector", lambda e: e.bn_aggr(STATS[:, 12:14], STATS[:, 0:12]),
               reads=("STATS",), writes=("STATS",))
        S.emit("scalar", lambda e: e.activation(
            STATS[:, 15:16], STATS[:, 13:14], AF.Ln, bias=CONSTS[:, 24:25]),
            reads=("STATS", "CONSTS"), writes=("STATS",))
        S.emit("scalar", lambda e: e.activation(
            STATS[:, 14:15], STATS[:, 15:16], AF.Exp, scale=-0.5),
            reads=("STATS",), writes=("STATS",))
        if apply_eng == "vector":
            S.emit("vector",
                   (lambda o, tb, xbuf, XH: (lambda e: e.tensor_scalar(
                       XH[tb][:, :], XRES[xbuf][:, o:o + D],
                       STATS[:, 12:13], STATS[:, 14:15], ALU.subtract, ALU.mult)))(o, tb, xbuf, XH),
                   reads=(xbuf_k(xbuf), "STATS"), writes=(f"{hkey}{tb}",))
        else:
            S.emit("scalar", lambda e: e.mul(STATS[:, 15:16], STATS[:, 12:13],
                                             STATS[:, 14:15]),
                   reads=("STATS",), writes=("STATS",))
            S.emit("scalar", lambda e: e.mul(STATS[:, 15:16], STATS[:, 15:16], -1.0),
                   reads=("STATS",), writes=("STATS",))
            S.emit("scalar",
                   (lambda o, tb, xbuf, XH: (lambda e: e.activation(
                       XH[tb][:, :], XRES[xbuf][:, o:o + D], AF.Identity,
                       bias=STATS[:, 15:16], scale=STATS[:, 14:15])))(o, tb, xbuf, XH),
                   reads=(xbuf_k(xbuf), "STATS"), writes=(f"{hkey}{tb}",))

    def h_group(ec):
        po = ec % 4
        S.emit("tensor",
               (lambda ec, po: (lambda e: e.matmul(
                   PMM[po][:, 0:TS], BBINROW[0:1, ec * 128:(ec + 1) * 128],
                   ONES[0:1, 0:256], start=True, stop=False)))(ec, po),
               reads=("BBINROW", "ONES"), writes=(f"PMM{po}",))
        for dc in range(NDC):
            S.emit("tensor",
                   (lambda ec, dc, po: (lambda e: e.matmul(
                       PMM[po][:, 0:TS],
                       W_INS[:, dc * D + ec * 128: dc * D + (ec + 1) * 128],
                       XTA[:, dc * TS:(dc + 1) * TS],
                       start=False, stop=(dc == NDC - 1))))(ec, dc, po),
                   reads=("W_INS", f"XTAd{dc}_0", f"XTAd{dc}_1"), writes=(f"PMM{po}",))

    def attn_a(st, seq_pos, nst):
        S.emit("scalar", lambda e: e.activation(DUM[:, :], CONSTS[:, 24:25], AF.Ln),
               reads=("CONSTS",), writes=("DUM",))
        h_group(0)
        for ec in range(NDC):
            po = ec % 4
            if ec + 1 < NDC:
                h_group(ec + 1)
            init = 0.0 if seq_pos == 0 else STATE[:, ec:ec + 1]
            rd = (f"PMM{po}", "ABC") if seq_pos == 0 else (f"PMM{po}", "ABC", f"STATE{ec}")
            S.emit("vector",
                   (lambda ec, po, init: (lambda e: e.tensor_tensor_scan(
                       YBF[:, ec * TS:(ec + 1) * TS], ABC[:, ec * TS:(ec + 1) * TS],
                       PMM[po][:, 0:TS], init, ALU.mult, ALU.add)))(ec, po, init),
                   reads=rd, writes=(f"YBF{ec}",))
        if seq_pos != per_seq - 1:
            ybf3 = YBF[:, :].rearrange("p (n t) -> p n t", t=TS)
            S.emit("vector",
                   (lambda ybf3: (lambda e: e.tensor_copy(
                       STATE[:, 0:NDC], ybf3[:, :, TS - 1:TS])))(ybf3),
                   reads=tuple(f"YBF{ec}" for ec in range(NDC)),
                   writes=tuple(f"STATE{ec}" for ec in range(NDC)))
        if st + 1 < nst:
            ln_stats((st + 1) % 3, 0, XHA, "XHA")
            ln_stats((st + 1) % 3, 1, XHA, "XHA")

    def attn_b(xbuf, tr_next):
        for tb in range(2):
            for eh in range(2):
                S.emit("tensor",
                       (lambda tb, eh: (lambda e: e.matmul(
                           POUT[2 * tb + eh][:, :], ONES[0:1, 0:128],
                           BOUT[0:1, eh * 512:(eh + 1) * 512],
                           start=True, stop=False)))(tb, eh),
                       reads=("ONES", "BOUT"), writes=(f"POUT{2 * tb + eh}",))
            for ec in range(NDC):
                for eh in range(2):
                    S.emit("tensor",
                           (lambda ec, tb, eh: (lambda e: e.matmul(
                               POUT[2 * tb + eh][:, :],
                               YBF[:, ec * TS + tb * TB: ec * TS + (tb + 1) * TB],
                               W_OUT[:, ec * D + eh * 512: ec * D + (eh + 1) * 512],
                               start=False, stop=(ec == NDC - 1))))(ec, tb, eh),
                           reads=(f"YBF{ec}", "W_OUT"), writes=(f"POUT{2 * tb + eh}",))
            for eh in range(2):
                S.emit("vector",
                       (lambda tb, eh, xbuf: (lambda e: e.tensor_add(
                           XRES[xbuf][:, tb * D + eh * 512: tb * D + (eh + 1) * 512],
                           XRES[xbuf][:, tb * D + eh * 512: tb * D + (eh + 1) * 512],
                           POUT[2 * tb + eh][:, :])))(tb, eh, xbuf),
                       reads=(xbuf_k(xbuf), f"POUT{2 * tb + eh}"), writes=(xbuf_k(xbuf),))
                S.emit("vector",
                       (lambda tb, eh, xbuf: (lambda e: e.bn_stats(
                           STATS[:, 6 * eh:6 * eh + 6],
                           XRES[xbuf][:, tb * D + eh * 512: tb * D + (eh + 1) * 512])))(tb, eh, xbuf),
                       reads=(xbuf_k(xbuf),), writes=("STATS",))
            ln_stats_tail(xbuf, tb, XHF, "XHF", "scalar")
        gelu_preload()
        pe_trF()
        if tr_next:
            dma_tr_next()

    def gelu_preload():
        S.emit("scalar", lambda e: e.activation(DUM[:, :], CONSTS[:, 24:25], AF.Gelu),
               reads=("CONSTS",), writes=("DUM",))

    def dma_tr_next():
        tot = S.semv.get("DTR", 0) + 16 * 2 * NDC
        for tb in range(2):
            for dc in range(NDC):
                S.emit("sync",
                       (lambda tb, dc: (lambda e: e.dma_start_transpose(
                           XTA[:, dc * TS + tb * TB: dc * TS + (tb + 1) * TB],
                           XHA[tb][:, dc * 128:(dc + 1) * 128])))(tb, dc),
                       reads=(f"XHA{tb}",), writes=(f"XTAd{dc}_{tb}",),
                       sem="DTR", inc=16, force_token_val=tot)

    def pe_trF():
        for k in range(2 * NDC):
            tb, dc = k // NDC, k % NDC
            j, sl = k % 4, (k // 4) % 2
            dst = PMM[j][:, TS + sl * 128: TS + (sl + 1) * 128]
            S.emit("tensor",
                   (lambda tb, dc, dst: (lambda e: e.transpose(
                       dst, XHF[tb][:, dc * 128:(dc + 1) * 128], EYEF[:, :])))(tb, dc, dst),
                   reads=(f"XHF{tb}", "EYEF"), writes=(f"PMM{j}",))
            S.emit("scalar",
                   (lambda tb, dc, dst: (lambda e: e.copy(
                       XTF[:, dc * TS + tb * TB: dc * TS + (tb + 1) * TB], dst)))(tb, dc, dst),
                   reads=(f"PMM{j}",), writes=(f"XTFd{dc}_{tb}",))

    def w_group_load(st, g):
        sl = g % 4
        S.emit("sync",
               (lambda g, sl: (lambda e: e.dma_start(
                   WSTR[sl][:, :],
                   w_bf[g * G:(g + 1) * G].rearrange("c f e -> f c e"))))(g, sl),
               reads=(f"WBF{g}",), writes=(f"WSTR{sl}",), sem=f"DW{sl}", inc=16)

    def ffn(st, xbuf):
        nsl = 2 if st == 0 else 4
        pref = 2 if st == 0 else 4

        def cproj(i):
            c = FFN_ORDER[i]
            g, cc, po = i // G, i % G, i % 8
            sl = g % nsl
            for tb in range(2):
                for eh in range(2):
                    S.emit("tensor",
                           (lambda i, cc, sl, po, tb, eh: (lambda e: e.matmul(
                               POUT[2 * tb + eh][:, :],
                               HC[po][:, tb * TB:(tb + 1) * TB],
                               WSTR[sl][:, cc * D + eh * 512: cc * D + (eh + 1) * 512],
                               start=(i == 0), stop=(i == NCH - 1))))(i, cc, sl, po, tb, eh),
                           reads=(f"HC{po}", f"WSTR{sl}"), writes=(f"POUT{2 * tb + eh}",))
            if cc == G - 1 and g + pref < NG:
                if st == 0:
                    conv_group(g + pref)
                else:
                    w_group_load(st, g + pref)

        def zmm(i):
            c = FFN_ORDER[i]
            r = c // 2
            g2 = (r % 8) // 2
            slot = ((r // 8) * 2 + ((r % 8) % 2)) * 2 + (c % 2)
            po = i % 4
            pap = PMM[po][:, 0:TS]
            pkey = f"PMM{po}"
            S.emit("tensor",
                   (lambda c, g2, slot, pap: (lambda e: e.matmul(
                       pap,
                       M2Z[g2 * 32:(g2 + 1) * 32, slot * 128:(slot + 1) * 128],
                       XTF[g2 * 32:(g2 + 1) * 32, (c // 16) * TS:((c // 16) + 1) * TS],
                       start=True, stop=True,
                       tile_position=(g2 * 32, 0))))(c, g2, slot, pap),
                   reads=("M2Z", f"XTFd{c // 16}_0", f"XTFd{c // 16}_1"),
                   writes=(pkey,))
            S.emit("scalar",
                   (lambda pap, hpo: (lambda e: e.activation(
                       HC[hpo][:, :], pap, AF.Gelu)))(pap, i % 8),
                   reads=(pkey,), writes=(f"HC{i % 8}",))

        done = 0
        for k in range(NCH // 4):
            for j in range(4):
                zmm(4 * k + j)
            while done < 4 * k:
                cproj(done)
                done += 1
        while done < NCH:
            cproj(done)
            done += 1
        for tb in range(2):
            for eh in range(2):
                S.emit("vector",
                       (lambda tb, eh, xbuf: (lambda e: e.tensor_add(
                           XRES[xbuf][:, tb * D + eh * 512: tb * D + (eh + 1) * 512],
                           XRES[xbuf][:, tb * D + eh * 512: tb * D + (eh + 1) * 512],
                           POUT[2 * tb + eh][:, :])))(tb, eh, xbuf),
                       reads=(xbuf_k(xbuf), f"POUT{2 * tb + eh}"), writes=(xbuf_k(xbuf),))

    def x_store(st):
        sl = st % 3
        S.emit("gpsimd",
               (lambda st, sl: (lambda e: e.dma_start(
                   y_out[st * TS:(st + 1) * TS, :].rearrange("(a p) e -> p a e", p=128),
                   XRES[sl][:, :])))(st, sl),
               reads=(f"XRES{sl}",), writes=(), sem=f"DXO{sl}", inc=16)

    x_load(0)
    ln_stats(0, 0, XHA, "XHA")
    ln_stats(0, 1, XHA, "XHA")
    conv_group(0)
    conv_group(1)
    x_load(1)
    for st in range(nst):
        xbuf = st % 3
        seq_pos = st % per_seq
        if st > 0:
            w_group_load(st, 0)
            w_group_load(st, 1)
            w_group_load(st, 2)
            w_group_load(st, 3)
        if st == 0:
            dma_tr_next()
        attn_a(st, seq_pos, nst)
        attn_b(xbuf, st + 1 < nst)
        ffn(st, xbuf)
        if st + 2 < nst:
            x_load(st + 2)
        x_store(st)

    S.raw_wait("gpsimd", "DXO0", S.semv.get("DXO0", 0))
    S.raw_wait("gpsimd", "DXO1", S.semv.get("DXO1", 0))
    S.raw_wait("gpsimd", "DXO2", S.semv.get("DXO2", 0))

    def replay(name):
        def f(eng):
            for waits, fn, sem, inc in S.ops[name]:
                for s, v in waits:
                    eng.wait_ge(sems[s], v)
                if fn is not None:
                    fn(eng).then_inc(sems[sem], inc)
        return f

    with nc.Block() as block:
        block.sync(replay("sync"))
        block.gpsimd(replay("gpsimd"))
        block.tensor(replay("tensor"))
        block.vector(replay("vector"))
        block.scalar(replay("scalar"))

    ctx.close()
    return nc


def host_stage(L0_attn_w_in, L0_attn_b_in, L0_attn_a, L0_attn_b, L0_attn_w_out,
               L0_attn_b_out, L0_ffn_rail_maker, L0_ffn_rail_up_proj, L0_ffn_w_cproj):
    f32 = np.float32
    w_inT = np.ascontiguousarray(np.asarray(L0_attn_w_in, f32).T).reshape(NDC, 128, D)
    w_outT = np.ascontiguousarray(np.asarray(L0_attn_w_out, f32).T).reshape(NDC, 128, D)
    consts = np.zeros((128, 32), f32)
    consts[:, 0:8] = np.asarray(L0_attn_a, f32).reshape(8, 128).T
    consts[:, 8:16] = np.asarray(L0_attn_b, f32).reshape(8, 128).T
    consts[:, 16:24] = np.asarray(L0_attn_b_in, f32).reshape(8, 128).T
    consts[:, 24] = 1e-5
    b_out_h = np.ascontiguousarray(np.asarray(L0_attn_b_out, f32)[None, :])
    ones_h = np.ones((1, 256), f32)
    b_row_h = np.ascontiguousarray(np.asarray(L0_attn_b, f32)[None, :])
    bin_row_h = np.ascontiguousarray(np.asarray(L0_attn_b_in, f32)[None, :])
    eye_h = np.eye(128, dtype=f32)
    mk = np.asarray(L0_ffn_rail_maker, f32)
    maker_h = np.zeros((16, 8192), f32)
    for h in range(4):
        for r in range(64):
            sub = (r % 8) % 2
            c0 = (h * 64 + r) * 32 + sub * 16
            maker_h[:, c0:c0 + 16] = mk[h, r].T
    maker_h = np.ascontiguousarray(maker_h)
    up_h = np.ascontiguousarray(
        np.asarray(L0_ffn_rail_up_proj, f32).transpose(3, 0, 1, 2).reshape(16, FH))
    wt = np.ascontiguousarray(np.asarray(L0_ffn_w_cproj, f32).T)
    wt = wt.reshape(2, 2, 64, 64, D)
    wt = wt.transpose(2, 0, 1, 3, 4)
    w_arr = np.ascontiguousarray(wt.reshape(NCH, 128, D)[FFN_ORDER])
    return dict(w_inT=w_inT, w_outT=w_outT, consts=consts, b_out_h=b_out_h,
                ones_h=ones_h, b_row_h=b_row_h, bin_row_h=bin_row_h,
                eye_h=eye_h, maker_h=maker_h, up_h=up_h, w_arr=w_arr)


def kernel(L0_attn_x, L0_attn_w_in, L0_attn_b_in, L0_attn_a, L0_attn_b,
           L0_attn_w_out, L0_attn_b_out, L0_ffn_rail_maker, L0_ffn_rail_up_proj,
           L0_ffn_w_cproj):
    x = np.ascontiguousarray(np.asarray(L0_attn_x, np.float32))
    B, T, _ = x.shape
    nseq = B // NCORES
    weights = host_stage(L0_attn_w_in, L0_attn_b_in, L0_attn_a, L0_attn_b,
                         L0_attn_w_out, L0_attn_b_out, L0_ffn_rail_maker,
                         L0_ffn_rail_up_proj, L0_ffn_w_cproj)
    nc = build_nc(nseq, T)
    in_maps = []
    for core in range(NCORES):
        m = dict(weights)
        m["x_in"] = np.ascontiguousarray(
            x[core * nseq:(core + 1) * nseq].reshape(nseq * T, D))
        in_maps.append(m)
    res = run_bass_kernel_spmd(nc, in_maps, core_ids=list(range(NCORES)))
    outs = [np.asarray(r["y_out"]).reshape(nseq, T, D) for r in res.results]
    return np.concatenate(outs, axis=0)


# version 19
# speedup vs baseline: 1.0163x; 1.0050x over previous
import numpy as np
import concourse.bass as bass
import concourse.mybir as mybir
from concourse.bass_utils import run_bass_kernel_spmd

F32 = mybir.dt.float32
BF16 = mybir.dt.bfloat16
AF = mybir.ActivationFunctionType
ALU = mybir.AluOpType

NCORES = 8
D = 1024
NDC = D // 128
FH = 16384
NCH = FH // 128
TS = 256
TB = 128
G = 8
NG = NCH // G

ENGINES = ("tensor", "vector", "scalar", "sync", "gpsimd")

_BANDS = [[c for c in range(NCH) if ((c // 2) % 8) // 2 == b] for b in range(4)]
FFN_ORDER = [c for group in zip(*_BANDS) for c in group]
assert sorted(FFN_ORDER) == list(range(NCH))


class Sched:

    def __init__(self):
        self.ops = {e: [] for e in ENGINES}
        self.semv = {}
        self.lastw = {}
        self.readers = {}
        self.ewaits = {e: {} for e in ENGINES}

    def emit(self, eng, fn, reads=(), writes=(), sem=None, inc=None,
             force_token_val=None):
        if sem is None:
            sem, inc = "E_" + eng, 1
        waits = {}

        def addw(toks):
            if not toks:
                return
            for s, v in toks.items():
                if eng == "tensor" and s == "E_tensor":
                    continue
                if self.ewaits[eng].get(s, 0) >= v:
                    continue
                waits[s] = max(waits.get(s, 0), v)

        for b in reads:
            addw(self.lastw.get(b))
        for b in writes:
            addw(self.lastw.get(b))
            addw(self.readers.get(b))
        for s, v in waits.items():
            self.ewaits[eng][s] = v
        self.semv[sem] = self.semv.get(sem, 0) + inc
        tok = (sem, self.semv[sem] if force_token_val is None else force_token_val)
        for b in writes:
            self.lastw[b] = {tok[0]: tok[1]}
            self.readers[b] = {}
        for b in reads:
            if b not in writes:
                r = self.readers.setdefault(b, {})
                r[tok[0]] = max(r.get(tok[0], 0), tok[1])
        self.ops[eng].append((sorted(waits.items()), fn, sem, inc))
        return tok

    def raw_wait(self, eng, sem, val):
        if self.ewaits[eng].get(sem, 0) >= val:
            return
        self.ewaits[eng][sem] = val
        self.ops[eng].append(((("" + sem, val),), None, None, 0))


def build_nc(nseq, tseq):
    rows = nseq * tseq
    nst = rows // TS
    per_seq = tseq // TS
    nc = bass.Bass("TRN2", target_bir_lowering=False)

    x_in = nc.dram_tensor("x_in", [rows, D], F32, kind="ExternalInput").ap()
    w_inT = nc.dram_tensor("w_inT", [NDC, 128, D], F32, kind="ExternalInput").ap()
    w_outT = nc.dram_tensor("w_outT", [NDC, 128, D], F32, kind="ExternalInput").ap()
    consts = nc.dram_tensor("consts", [128, 32], F32, kind="ExternalInput").ap()
    b_out_h = nc.dram_tensor("b_out_h", [1, D], F32, kind="ExternalInput").ap()
    ones_h = nc.dram_tensor("ones_h", [1, 256], F32, kind="ExternalInput").ap()
    b_row_h = nc.dram_tensor("b_row_h", [1, D], F32, kind="ExternalInput").ap()
    bin_row_h = nc.dram_tensor("bin_row_h", [1, D], F32, kind="ExternalInput").ap()
    eye_h = nc.dram_tensor("eye_h", [128, 128], F32, kind="ExternalInput").ap()
    maker_h = nc.dram_tensor("maker_h", [16, 8192], F32, kind="ExternalInput").ap()
    up_h = nc.dram_tensor("up_h", [16, FH], F32, kind="ExternalInput").ap()
    w_arr = nc.dram_tensor("w_arr", [NCH, 128, D], F32, kind="ExternalInput").ap()
    y_out = nc.dram_tensor("y_out", [rows, D], F32, kind="ExternalOutput").ap()
    w_bf = nc.dram_tensor("w_bf", [NCH, 128, D], BF16).ap()

    from contextlib import ExitStack
    ctx = ExitStack()

    def sb(name, shape, dt):
        return ctx.enter_context(nc.sbuf_tensor(name, shape, dt))

    def ps(name, shape):
        return ctx.enter_context(nc.psum_tensor(name, shape, F32))

    W_OUT = sb("W_OUT", [128, NDC * D], BF16)
    M2Z = sb("M2Z", [128, 4096], BF16)
    EYE = sb("EYE", [128, 128], BF16)
    EYEF = sb("EYEF", [128, 128], F32)
    ONES = sb("ONES", [1, 256], BF16)
    BROW = sb("BROW", [1, D], BF16)
    BINROW = sb("BINROW", [1, D], BF16)
    BBINROW = sb("BBINROW", [1, D], BF16)
    W_INS = sb("W_INS", [128, NDC * D], BF16)
    BB = sb("BB", [128, D], F32)
    BOUT = sb("BOUT", [1, D], BF16)
    CONSTS = sb("CONSTS", [128, 32], F32)
    AVEC = sb("AVEC", [128, 8], F32)
    BBIN = sb("BBIN", [128, 8], F32)
    ABC = sb("ABC", [128, NDC * TS], F32)
    STATE = sb("STATE", [128, NDC], F32)
    STATS = sb("STATS", [128, 16], F32)
    STG_M = sb("STG_M", [16, 2048], F32)
    STG_U = sb("STG_U", [16, 4096], F32)
    XRES = [sb(f"XRES{i}", [128, 2 * D], F32) for i in range(3)]
    XHA = [sb(f"XHA{i}", [128, D], BF16) for i in range(2)]
    XHF = [sb(f"XHF{i}", [128, D], F32) for i in range(2)]
    XTA = sb("XTA", [128, NDC * TS], BF16)
    XTF = sb("XTF", [128, NDC * TS], BF16)
    YBF = sb("YBF", [128, NDC * TS], BF16)
    WSTR = [sb(f"WSTR{i}", [128, G * D], BF16) for i in range(4)]
    HC = [sb(f"HC{i}", [128, TS], BF16) for i in range(8)]
    DUM = sb("DUM", [128, 1], BF16)

    POUT = [ps(f"POUT{i}", [128, 512]) for i in range(4)]
    PMM = [ps(f"PMM{i}", [128, 512]) for i in range(4)]

    sem_names = ["E_tensor", "E_vector", "E_scalar", "DSET", "DSTG_M", "DSTG_U",
                 "DX0", "DX1", "DX2", "DXO0", "DXO1", "DXO2", "DW0", "DW1", "DW2", "DW3", "DTR",
                 "DTF0", "DTF1", "DTF2", "DTF3", "DTF4", "DTF5", "DTF6", "DTF7",
                 "DCL0", "DCL1", "DCS0", "DCS1"]
    sems = {n: ctx.enter_context(nc.semaphore(n)) for n in sem_names}

    S = Sched()

    setup_loads = [
        (WSTR[0][:, :], w_inT.rearrange("a p f -> p a f"), "WSTR0"),
        (W_OUT[:, :], w_outT.rearrange("a p f -> p a f"), "W_OUT"),
        (CONSTS[:, :], consts, "CONSTS"),
        (BOUT[:, :], b_out_h, "BOUT"),
        (ONES[:, :], ones_h, "ONES"),
        (BROW[:, :], b_row_h, "BROW"),
        (BINROW[:, :], bin_row_h, "BINROW"),
        (EYE[:, :], eye_h, "EYE"),
        (EYEF[:, :], eye_h, "EYEF"),
    ]
    tot = 16 * len(setup_loads)
    for dst, src, buf in setup_loads:
        S.emit("gpsimd", (lambda d, s: (lambda e: e.dma_start(d, s)))(dst, src),
               reads=(), writes=(buf,), sem="DSET", inc=16, force_token_val=tot)

    S.emit("scalar", lambda e: e.activation(AVEC[:, :], CONSTS[:, 0:8], AF.Sigmoid),
           reads=("CONSTS",), writes=("AVEC",))
    S.emit("vector", lambda e: e.tensor_mul(BBIN[:, :], CONSTS[:, 8:16], CONSTS[:, 16:24]),
           reads=("CONSTS",), writes=("BBIN",))
    S.emit("vector", lambda e: e.memset(ABC[:, :], 1.0), writes=("ABC",))
    for ec in range(NDC):
        S.emit("vector",
               (lambda ec: (lambda e: e.tensor_scalar(
                   ABC[:, ec * TS:(ec + 1) * TS], ABC[:, ec * TS:(ec + 1) * TS],
                   AVEC[:, ec:ec + 1], None, ALU.mult)))(ec),
               reads=("ABC", "AVEC"), writes=("ABC",))
    S.emit("vector", lambda e: e.memset(M2Z[:, :], 0.0), writes=("M2Z",))
    S.emit("vector", lambda e: e.tensor_mul(BBINROW[:, :], BROW[:, :], BINROW[:, :]),
           reads=("BROW", "BINROW"), writes=("BBINROW",))
    for eh in range(2):
        S.emit("tensor",
               (lambda eh: (lambda e: e.matmul(
                   POUT[0][:, :], ONES[0:1, 0:128], BROW[0:1, eh * 512:(eh + 1) * 512],
                   start=True, stop=True)))(eh),
               reads=("ONES", "BROW"), writes=("POUT0",))
        S.emit("vector",
               (lambda eh: (lambda e: e.tensor_copy(
                   BB[:, eh * 512:(eh + 1) * 512], POUT[0][:, :])))(eh),
               reads=("POUT0",), writes=("BB",))
    for dc in range(NDC):
        S.emit("vector",
               (lambda dc: (lambda e: e.tensor_mul(
                   W_INS[:, dc * D:(dc + 1) * D], WSTR[0][:, dc * D:(dc + 1) * D],
                   BB[:, :])))(dc),
               reads=("WSTR0", "BB"), writes=("W_INS",))

    for h in range(4):
        S.emit("gpsimd",
               (lambda h: (lambda e: e.dma_start(
                   STG_M[:, :], maker_h[:, h * 2048:(h + 1) * 2048])))(h),
               reads=(), writes=("STG_M",), sem="DSTG_M", inc=16)
        S.emit("gpsimd",
               (lambda h: (lambda e: e.dma_start(
                   STG_U[:, :], up_h[:, h * 4096:(h + 1) * 4096])))(h),
               reads=(), writes=("STG_U",), sem="DSTG_U", inc=16)
        for r in range(64):
            pi = r % 2
            S.emit("tensor",
                   (lambda h, r, pi: (lambda e: e.matmul(
                       PMM[pi][0:32, 0:64],
                       STG_M[0:16, r * 32:(r + 1) * 32],
                       STG_U[0:16, r * 64:(r + 1) * 64],
                       start=True, stop=True)))(h, r, pi),
                   reads=("STG_M", "STG_U"), writes=(f"PMM{pi}",))
            g = r % 8
            g2, sub = g // 2, g % 2
            half, hp = h // 2, h % 2
            slot = ((r // 8) * 2 + sub) * 2 + half
            rr = g2 * 32
            cc0 = slot * 128 + hp * 64
            S.emit("vector",
                   (lambda pi, rr, cc0: (lambda e: e.tensor_copy(
                       M2Z[rr:rr + 32, cc0:cc0 + 64], PMM[pi][0:32, 0:64])))(pi, rr, cc0),
                   reads=(f"PMM{pi}",), writes=("M2Z",))

    def conv_group(g):
        sl = g % 2
        S.emit("gpsimd",
               (lambda g, sl: (lambda e: e.dma_start(
                   WSTR[sl][:, :],
                   w_arr[g * G:(g + 1) * G].rearrange("c f e -> f c e"))))(g, sl),
               reads=(), writes=(f"WSTR{sl}",), sem=f"DCL{sl}", inc=16)
        S.emit("sync",
               (lambda g, sl: (lambda e: e.dma_start(
                   w_bf[g * G:(g + 1) * G].rearrange("c f e -> f c e"),
                   WSTR[sl][:, :])))(g, sl),
               reads=(f"WSTR{sl}",), writes=(f"WBF{g}",), sem=f"DCS{sl}", inc=16)

    def x_load(st):
        sl = st % 3
        S.emit("gpsimd",
               (lambda st, sl: (lambda e: e.dma_start(
                   XRES[sl][:, :],
                   x_in[st * TS:(st + 1) * TS, :].rearrange("(a p) e -> p a e", p=128))))(st, sl),
               reads=(), writes=(f"XRES{sl}",), sem=f"DX{sl}", inc=16)

    def xbuf_k(sl):
        return f"XRES{sl}"

    def ln_stats(xbuf, tb, XH, hkey, apply_eng="vector"):
        o = tb * D
        S.emit("vector",
               (lambda o, xbuf: (lambda e: e.bn_stats(STATS[:, 0:6], XRES[xbuf][:, o:o + 512])))(o, xbuf),
               reads=(xbuf_k(xbuf),), writes=("STATS",))
        S.emit("vector",
               (lambda o, xbuf: (lambda e: e.bn_stats(STATS[:, 6:12], XRES[xbuf][:, o + 512:o + 1024])))(o, xbuf),
               reads=(xbuf_k(xbuf),), writes=("STATS",))
        ln_stats_tail(xbuf, tb, XH, hkey, apply_eng)

    def ln_stats_tail(xbuf, tb, XH, hkey, apply_eng="vector"):
        o = tb * D
        S.emit("vector", lambda e: e.bn_aggr(STATS[:, 12:14], STATS[:, 0:12]),
               reads=("STATS",), writes=("STATS",))
        S.emit("scalar", lambda e: e.activation(
            STATS[:, 15:16], STATS[:, 13:14], AF.Ln, bias=CONSTS[:, 24:25]),
            reads=("STATS", "CONSTS"), writes=("STATS",))
        S.emit("scalar", lambda e: e.activation(
            STATS[:, 14:15], STATS[:, 15:16], AF.Exp, scale=-0.5),
            reads=("STATS",), writes=("STATS",))
        if apply_eng == "vector":
            S.emit("vector",
                   (lambda o, tb, xbuf, XH: (lambda e: e.tensor_scalar(
                       XH[tb][:, :], XRES[xbuf][:, o:o + D],
                       STATS[:, 12:13], STATS[:, 14:15], ALU.subtract, ALU.mult)))(o, tb, xbuf, XH),
                   reads=(xbuf_k(xbuf), "STATS"), writes=(f"{hkey}{tb}",))
        else:
            S.emit("scalar", lambda e: e.mul(STATS[:, 15:16], STATS[:, 12:13],
                                             STATS[:, 14:15]),
                   reads=("STATS",), writes=("STATS",))
            S.emit("scalar", lambda e: e.mul(STATS[:, 15:16], STATS[:, 15:16], -1.0),
                   reads=("STATS",), writes=("STATS",))
            S.emit("scalar",
                   (lambda o, tb, xbuf, XH: (lambda e: e.activation(
                       XH[tb][:, :], XRES[xbuf][:, o:o + D], AF.Identity,
                       bias=STATS[:, 15:16], scale=STATS[:, 14:15])))(o, tb, xbuf, XH),
                   reads=(xbuf_k(xbuf), "STATS"), writes=(f"{hkey}{tb}",))

    def h_group(ec):
        po = ec % 4
        S.emit("tensor",
               (lambda ec, po: (lambda e: e.matmul(
                   PMM[po][:, 0:TS], BBINROW[0:1, ec * 128:(ec + 1) * 128],
                   ONES[0:1, 0:256], start=True, stop=False)))(ec, po),
               reads=("BBINROW", "ONES"), writes=(f"PMM{po}",))
        for dc in range(NDC):
            S.emit("tensor",
                   (lambda ec, dc, po: (lambda e: e.matmul(
                       PMM[po][:, 0:TS],
                       W_INS[:, dc * D + ec * 128: dc * D + (ec + 1) * 128],
                       XTA[:, dc * TS:(dc + 1) * TS],
                       start=False, stop=(dc == NDC - 1))))(ec, dc, po),
                   reads=("W_INS", f"XTAd{dc}_0", f"XTAd{dc}_1"), writes=(f"PMM{po}",))

    def attn_a(st, seq_pos, nst):
        S.emit("scalar", lambda e: e.activation(DUM[:, :], CONSTS[:, 24:25], AF.Ln),
               reads=("CONSTS",), writes=("DUM",))
        h_group(0)
        for ec in range(NDC):
            po = ec % 4
            if ec + 1 < NDC:
                h_group(ec + 1)
            init = 0.0 if seq_pos == 0 else STATE[:, ec:ec + 1]
            rd = (f"PMM{po}", "ABC") if seq_pos == 0 else (f"PMM{po}", "ABC", f"STATE{ec}")
            S.emit("vector",
                   (lambda ec, po, init: (lambda e: e.tensor_tensor_scan(
                       YBF[:, ec * TS:(ec + 1) * TS], ABC[:, ec * TS:(ec + 1) * TS],
                       PMM[po][:, 0:TS], init, ALU.mult, ALU.add)))(ec, po, init),
                   reads=rd, writes=(f"YBF{ec}",))
        if seq_pos != per_seq - 1:
            ybf3 = YBF[:, :].rearrange("p (n t) -> p n t", t=TS)
            S.emit("vector",
                   (lambda ybf3: (lambda e: e.tensor_copy(
                       STATE[:, 0:NDC], ybf3[:, :, TS - 1:TS])))(ybf3),
                   reads=tuple(f"YBF{ec}" for ec in range(NDC)),
                   writes=tuple(f"STATE{ec}" for ec in range(NDC)))
        if st + 1 < nst:
            ln_stats((st + 1) % 3, 0, XHA, "XHA")
            ln_stats((st + 1) % 3, 1, XHA, "XHA")

    def attn_b(xbuf, tr_next):
        for tb in range(2):
            for eh in range(2):
                S.emit("tensor",
                       (lambda tb, eh: (lambda e: e.matmul(
                           POUT[2 * tb + eh][:, :], ONES[0:1, 0:128],
                           BOUT[0:1, eh * 512:(eh + 1) * 512],
                           start=True, stop=False)))(tb, eh),
                       reads=("ONES", "BOUT"), writes=(f"POUT{2 * tb + eh}",))
            for eh in range(2):
                for ec in range(NDC):
                    S.emit("tensor",
                           (lambda ec, tb, eh: (lambda e: e.matmul(
                               POUT[2 * tb + eh][:, :],
                               YBF[:, ec * TS + tb * TB: ec * TS + (tb + 1) * TB],
                               W_OUT[:, ec * D + eh * 512: ec * D + (eh + 1) * 512],
                               start=False, stop=(ec == NDC - 1))))(ec, tb, eh),
                           reads=(f"YBF{ec}", "W_OUT"), writes=(f"POUT{2 * tb + eh}",))
            for eh in range(2):
                S.emit("vector",
                       (lambda tb, eh, xbuf: (lambda e: e.tensor_add(
                           XRES[xbuf][:, tb * D + eh * 512: tb * D + (eh + 1) * 512],
                           XRES[xbuf][:, tb * D + eh * 512: tb * D + (eh + 1) * 512],
                           POUT[2 * tb + eh][:, :])))(tb, eh, xbuf),
                       reads=(xbuf_k(xbuf), f"POUT{2 * tb + eh}"), writes=(xbuf_k(xbuf),))
                S.emit("vector",
                       (lambda tb, eh, xbuf: (lambda e: e.bn_stats(
                           STATS[:, 6 * eh:6 * eh + 6],
                           XRES[xbuf][:, tb * D + eh * 512: tb * D + (eh + 1) * 512])))(tb, eh, xbuf),
                       reads=(xbuf_k(xbuf),), writes=("STATS",))
            ln_stats_tail(xbuf, tb, XHF, "XHF", "scalar")
        gelu_preload()
        pe_trF()
        if tr_next:
            dma_tr_next()

    def gelu_preload():
        S.emit("scalar", lambda e: e.activation(DUM[:, :], CONSTS[:, 24:25], AF.Gelu),
               reads=("CONSTS",), writes=("DUM",))

    def dma_tr_next():
        tot = S.semv.get("DTR", 0) + 16 * 2 * NDC
        for tb in range(2):
            for dc in range(NDC):
                S.emit("sync",
                       (lambda tb, dc: (lambda e: e.dma_start_transpose(
                           XTA[:, dc * TS + tb * TB: dc * TS + (tb + 1) * TB],
                           XHA[tb][:, dc * 128:(dc + 1) * 128])))(tb, dc),
                       reads=(f"XHA{tb}",), writes=(f"XTAd{dc}_{tb}",),
                       sem="DTR", inc=16, force_token_val=tot)

    def pe_trF():
        for k in range(2 * NDC):
            tb, dc = k // NDC, k % NDC
            j, sl = k % 4, (k // 4) % 2
            dst = PMM[j][:, TS + sl * 128: TS + (sl + 1) * 128]
            S.emit("tensor",
                   (lambda tb, dc, dst: (lambda e: e.transpose(
                       dst, XHF[tb][:, dc * 128:(dc + 1) * 128], EYEF[:, :])))(tb, dc, dst),
                   reads=(f"XHF{tb}", "EYEF"), writes=(f"PMM{j}",))
            S.emit("scalar",
                   (lambda tb, dc, dst: (lambda e: e.copy(
                       XTF[:, dc * TS + tb * TB: dc * TS + (tb + 1) * TB], dst)))(tb, dc, dst),
                   reads=(f"PMM{j}",), writes=(f"XTFd{dc}_{tb}",))

    def w_group_load(st, g):
        sl = g % 4
        S.emit("sync",
               (lambda g, sl: (lambda e: e.dma_start(
                   WSTR[sl][:, :],
                   w_bf[g * G:(g + 1) * G].rearrange("c f e -> f c e"))))(g, sl),
               reads=(f"WBF{g}",), writes=(f"WSTR{sl}",), sem=f"DW{sl}", inc=16)

    def ffn(st, xbuf):
        nsl = 2 if st == 0 else 4
        pref = 2 if st == 0 else 4

        def cproj(i):
            c = FFN_ORDER[i]
            g, cc, po = i // G, i % G, i % 8
            sl = g % nsl
            for tb in range(2):
                for eh in range(2):
                    S.emit("tensor",
                           (lambda i, cc, sl, po, tb, eh: (lambda e: e.matmul(
                               POUT[2 * tb + eh][:, :],
                               HC[po][:, tb * TB:(tb + 1) * TB],
                               WSTR[sl][:, cc * D + eh * 512: cc * D + (eh + 1) * 512],
                               start=(i == 0), stop=(i == NCH - 1))))(i, cc, sl, po, tb, eh),
                           reads=(f"HC{po}", f"WSTR{sl}"), writes=(f"POUT{2 * tb + eh}",))
            if cc == G - 1 and g + pref < NG:
                if st == 0:
                    conv_group(g + pref)
                else:
                    w_group_load(st, g + pref)

        def zmm(i):
            c = FFN_ORDER[i]
            r = c // 2
            g2 = (r % 8) // 2
            slot = ((r // 8) * 2 + ((r % 8) % 2)) * 2 + (c % 2)
            po = i % 4
            pap = PMM[po][:, 0:TS]
            pkey = f"PMM{po}"
            S.emit("tensor",
                   (lambda c, g2, slot, pap: (lambda e: e.matmul(
                       pap,
                       M2Z[g2 * 32:(g2 + 1) * 32, slot * 128:(slot + 1) * 128],
                       XTF[g2 * 32:(g2 + 1) * 32, (c // 16) * TS:((c // 16) + 1) * TS],
                       start=True, stop=True,
                       tile_position=(g2 * 32, 0))))(c, g2, slot, pap),
                   reads=("M2Z", f"XTFd{c // 16}_0", f"XTFd{c // 16}_1"),
                   writes=(pkey,))
            S.emit("scalar",
                   (lambda pap, hpo: (lambda e: e.activation(
                       HC[hpo][:, :], pap, AF.Gelu)))(pap, i % 8),
                   reads=(pkey,), writes=(f"HC{i % 8}",))

        done = 0
        for k in range(NCH // 4):
            for j in range(4):
                zmm(4 * k + j)
            while done < 4 * k:
                cproj(done)
                done += 1
        while done < NCH:
            cproj(done)
            done += 1
        for tb in range(2):
            for eh in range(2):
                S.emit("vector",
                       (lambda tb, eh, xbuf: (lambda e: e.tensor_add(
                           XRES[xbuf][:, tb * D + eh * 512: tb * D + (eh + 1) * 512],
                           XRES[xbuf][:, tb * D + eh * 512: tb * D + (eh + 1) * 512],
                           POUT[2 * tb + eh][:, :])))(tb, eh, xbuf),
                       reads=(xbuf_k(xbuf), f"POUT{2 * tb + eh}"), writes=(xbuf_k(xbuf),))

    def x_store(st):
        sl = st % 3
        S.emit("gpsimd",
               (lambda st, sl: (lambda e: e.dma_start(
                   y_out[st * TS:(st + 1) * TS, :].rearrange("(a p) e -> p a e", p=128),
                   XRES[sl][:, :])))(st, sl),
               reads=(f"XRES{sl}",), writes=(), sem=f"DXO{sl}", inc=16)

    x_load(0)
    ln_stats(0, 0, XHA, "XHA")
    ln_stats(0, 1, XHA, "XHA")
    conv_group(0)
    conv_group(1)
    x_load(1)
    for st in range(nst):
        xbuf = st % 3
        seq_pos = st % per_seq
        if st > 0:
            w_group_load(st, 0)
            w_group_load(st, 1)
            w_group_load(st, 2)
            w_group_load(st, 3)
        if st == 0:
            dma_tr_next()
        attn_a(st, seq_pos, nst)
        attn_b(xbuf, st + 1 < nst)
        ffn(st, xbuf)
        if st + 2 < nst:
            x_load(st + 2)
        x_store(st)

    S.raw_wait("gpsimd", "DXO0", S.semv.get("DXO0", 0))
    S.raw_wait("gpsimd", "DXO1", S.semv.get("DXO1", 0))
    S.raw_wait("gpsimd", "DXO2", S.semv.get("DXO2", 0))

    def replay(name):
        def f(eng):
            for waits, fn, sem, inc in S.ops[name]:
                for s, v in waits:
                    eng.wait_ge(sems[s], v)
                if fn is not None:
                    fn(eng).then_inc(sems[sem], inc)
        return f

    with nc.Block() as block:
        block.sync(replay("sync"))
        block.gpsimd(replay("gpsimd"))
        block.tensor(replay("tensor"))
        block.vector(replay("vector"))
        block.scalar(replay("scalar"))

    ctx.close()
    return nc


def host_stage(L0_attn_w_in, L0_attn_b_in, L0_attn_a, L0_attn_b, L0_attn_w_out,
               L0_attn_b_out, L0_ffn_rail_maker, L0_ffn_rail_up_proj, L0_ffn_w_cproj):
    f32 = np.float32
    w_inT = np.ascontiguousarray(np.asarray(L0_attn_w_in, f32).T).reshape(NDC, 128, D)
    w_outT = np.ascontiguousarray(np.asarray(L0_attn_w_out, f32).T).reshape(NDC, 128, D)
    consts = np.zeros((128, 32), f32)
    consts[:, 0:8] = np.asarray(L0_attn_a, f32).reshape(8, 128).T
    consts[:, 8:16] = np.asarray(L0_attn_b, f32).reshape(8, 128).T
    consts[:, 16:24] = np.asarray(L0_attn_b_in, f32).reshape(8, 128).T
    consts[:, 24] = 1e-5
    b_out_h = np.ascontiguousarray(np.asarray(L0_attn_b_out, f32)[None, :])
    ones_h = np.ones((1, 256), f32)
    b_row_h = np.ascontiguousarray(np.asarray(L0_attn_b, f32)[None, :])
    bin_row_h = np.ascontiguousarray(np.asarray(L0_attn_b_in, f32)[None, :])
    eye_h = np.eye(128, dtype=f32)
    mk = np.asarray(L0_ffn_rail_maker, f32)
    maker_h = np.zeros((16, 8192), f32)
    for h in range(4):
        for r in range(64):
            sub = (r % 8) % 2
            c0 = (h * 64 + r) * 32 + sub * 16
            maker_h[:, c0:c0 + 16] = mk[h, r].T
    maker_h = np.ascontiguousarray(maker_h)
    up_h = np.ascontiguousarray(
        np.asarray(L0_ffn_rail_up_proj, f32).transpose(3, 0, 1, 2).reshape(16, FH))
    wt = np.ascontiguousarray(np.asarray(L0_ffn_w_cproj, f32).T)
    wt = wt.reshape(2, 2, 64, 64, D)
    wt = wt.transpose(2, 0, 1, 3, 4)
    w_arr = np.ascontiguousarray(wt.reshape(NCH, 128, D)[FFN_ORDER])
    return dict(w_inT=w_inT, w_outT=w_outT, consts=consts, b_out_h=b_out_h,
                ones_h=ones_h, b_row_h=b_row_h, bin_row_h=bin_row_h,
                eye_h=eye_h, maker_h=maker_h, up_h=up_h, w_arr=w_arr)


def kernel(L0_attn_x, L0_attn_w_in, L0_attn_b_in, L0_attn_a, L0_attn_b,
           L0_attn_w_out, L0_attn_b_out, L0_ffn_rail_maker, L0_ffn_rail_up_proj,
           L0_ffn_w_cproj):
    x = np.ascontiguousarray(np.asarray(L0_attn_x, np.float32))
    B, T, _ = x.shape
    nseq = B // NCORES
    weights = host_stage(L0_attn_w_in, L0_attn_b_in, L0_attn_a, L0_attn_b,
                         L0_attn_w_out, L0_attn_b_out, L0_ffn_rail_maker,
                         L0_ffn_rail_up_proj, L0_ffn_w_cproj)
    nc = build_nc(nseq, T)
    in_maps = []
    for core in range(NCORES):
        m = dict(weights)
        m["x_in"] = np.ascontiguousarray(
            x[core * nseq:(core + 1) * nseq].reshape(nseq * T, D))
        in_maps.append(m)
    res = run_bass_kernel_spmd(nc, in_maps, core_ids=list(range(NCORES)))
    outs = [np.asarray(r["y_out"]).reshape(nseq, T, D) for r in res.results]
    return np.concatenate(outs, axis=0)


# version 20
# speedup vs baseline: 1.0485x; 1.0316x over previous
import numpy as np
import concourse.bass as bass
import concourse.mybir as mybir
from concourse.bass_utils import run_bass_kernel_spmd

F32 = mybir.dt.float32
BF16 = mybir.dt.bfloat16
AF = mybir.ActivationFunctionType
ALU = mybir.AluOpType

NCORES = 8
D = 1024
NDC = D // 128
FH = 16384
NCH = FH // 128
TS = 256
TB = 128
G = 8
NG = NCH // G

ENGINES = ("tensor", "vector", "scalar", "sync", "gpsimd")

_BANDS = [[c for c in range(NCH) if ((c // 2) % 8) // 2 == b] for b in range(4)]
FFN_ORDER = [c for group in zip(*_BANDS) for c in group]
assert sorted(FFN_ORDER) == list(range(NCH))


class Sched:

    def __init__(self):
        self.ops = {e: [] for e in ENGINES}
        self.semv = {}
        self.lastw = {}
        self.readers = {}
        self.ewaits = {e: {} for e in ENGINES}

    def emit(self, eng, fn, reads=(), writes=(), sem=None, inc=None,
             force_token_val=None):
        if sem is None:
            sem, inc = "E_" + eng, 1
        waits = {}

        def addw(toks):
            if not toks:
                return
            for s, v in toks.items():
                if eng == "tensor" and s == "E_tensor":
                    continue
                if self.ewaits[eng].get(s, 0) >= v:
                    continue
                waits[s] = max(waits.get(s, 0), v)

        for b in reads:
            addw(self.lastw.get(b))
        for b in writes:
            addw(self.lastw.get(b))
            addw(self.readers.get(b))
        for s, v in waits.items():
            self.ewaits[eng][s] = v
        self.semv[sem] = self.semv.get(sem, 0) + inc
        tok = (sem, self.semv[sem] if force_token_val is None else force_token_val)
        for b in writes:
            self.lastw[b] = {tok[0]: tok[1]}
            self.readers[b] = {}
        for b in reads:
            if b not in writes:
                r = self.readers.setdefault(b, {})
                r[tok[0]] = max(r.get(tok[0], 0), tok[1])
        self.ops[eng].append((sorted(waits.items()), fn, sem, inc))
        return tok

    def raw_wait(self, eng, sem, val):
        if self.ewaits[eng].get(sem, 0) >= val:
            return
        self.ewaits[eng][sem] = val
        self.ops[eng].append(((("" + sem, val),), None, None, 0))


def build_nc(nseq, tseq):
    rows = nseq * tseq
    nst = rows // TS
    per_seq = tseq // TS
    nc = bass.Bass("TRN2", target_bir_lowering=False)

    x_in = nc.dram_tensor("x_in", [rows, D], F32, kind="ExternalInput").ap()
    w_inT = nc.dram_tensor("w_inT", [NDC, 128, D], F32, kind="ExternalInput").ap()
    w_outT = nc.dram_tensor("w_outT", [NDC, 128, D], F32, kind="ExternalInput").ap()
    consts = nc.dram_tensor("consts", [128, 32], F32, kind="ExternalInput").ap()
    b_out_h = nc.dram_tensor("b_out_h", [1, D], F32, kind="ExternalInput").ap()
    ones_h = nc.dram_tensor("ones_h", [1, 256], F32, kind="ExternalInput").ap()
    b_row_h = nc.dram_tensor("b_row_h", [1, D], F32, kind="ExternalInput").ap()
    bin_row_h = nc.dram_tensor("bin_row_h", [1, D], F32, kind="ExternalInput").ap()
    eye_h = nc.dram_tensor("eye_h", [128, 128], F32, kind="ExternalInput").ap()
    maker_h = nc.dram_tensor("maker_h", [16, 8192], F32, kind="ExternalInput").ap()
    up_h = nc.dram_tensor("up_h", [16, FH], F32, kind="ExternalInput").ap()
    w_arr = nc.dram_tensor("w_arr", [NCH, 128, D], F32, kind="ExternalInput").ap()
    y_out = nc.dram_tensor("y_out", [rows, D], F32, kind="ExternalOutput").ap()
    w_bf = nc.dram_tensor("w_bf", [NCH, 128, D], BF16).ap()

    from contextlib import ExitStack
    ctx = ExitStack()

    def sb(name, shape, dt):
        return ctx.enter_context(nc.sbuf_tensor(name, shape, dt))

    def ps(name, shape):
        return ctx.enter_context(nc.psum_tensor(name, shape, F32))

    W_OUT = sb("W_OUT", [128, NDC * D], BF16)
    M2Z = sb("M2Z", [128, 4096], BF16)
    EYE = sb("EYE", [128, 128], BF16)
    EYEF = sb("EYEF", [128, 128], F32)
    ONES = sb("ONES", [1, 256], BF16)
    BROW = sb("BROW", [1, D], BF16)
    BINROW = sb("BINROW", [1, D], BF16)
    BBINROW = sb("BBINROW", [1, D], BF16)
    W_INS = sb("W_INS", [128, NDC * D], BF16)
    BB = sb("BB", [128, D], F32)
    BOUT = sb("BOUT", [1, D], BF16)
    CONSTS = sb("CONSTS", [128, 32], F32)
    AVEC = sb("AVEC", [128, 8], F32)
    BBIN = sb("BBIN", [128, 8], F32)
    ABC = sb("ABC", [128, NDC * TS], F32)
    STATE = sb("STATE", [128, NDC], F32)
    STATS = sb("STATS", [128, 16], F32)
    STG_M = sb("STG_M", [16, 2048], F32)
    STG_U = sb("STG_U", [16, 4096], F32)
    XRES = [sb(f"XRES{i}", [128, 2 * D], F32) for i in range(4)]
    XHA = [sb(f"XHA{i}", [128, D], BF16) for i in range(2)]
    XHF = [sb(f"XHF{i}", [128, D], F32) for i in range(2)]
    XTA = sb("XTA", [128, NDC * TS], BF16)
    XTF = sb("XTF", [128, NDC * TS], BF16)
    YBF = sb("YBF", [128, NDC * TS], BF16)
    WSTR = [sb(f"WSTR{i}", [128, G * D], BF16) for i in range(3)]
    HC = [sb(f"HC{i}", [128, TS], BF16) for i in range(8)]
    DUM = sb("DUM", [128, 1], BF16)

    POUT = [ps(f"POUT{i}", [128, 512]) for i in range(4)]
    PMM = [ps(f"PMM{i}", [128, 512]) for i in range(4)]

    sem_names = ["E_tensor", "E_vector", "E_scalar", "DSET", "DSTG_M", "DSTG_U",
                 "DX0", "DX1", "DX2", "DX3", "DXO0", "DXO1", "DXO2", "DXO3", "DW0", "DW1", "DW2", "DW3", "DTR",
                 "DTF0", "DTF1", "DTF2", "DTF3", "DTF4", "DTF5", "DTF6", "DTF7",
                 "DCL0", "DCL1", "DCS0", "DCS1"]
    sems = {n: ctx.enter_context(nc.semaphore(n)) for n in sem_names}

    S = Sched()

    setup_loads = [
        (WSTR[0][:, :], w_inT.rearrange("a p f -> p a f"), "WSTR0"),
        (W_OUT[:, :], w_outT.rearrange("a p f -> p a f"), "W_OUT"),
        (CONSTS[:, :], consts, "CONSTS"),
        (BOUT[:, :], b_out_h, "BOUT"),
        (ONES[:, :], ones_h, "ONES"),
        (BROW[:, :], b_row_h, "BROW"),
        (BINROW[:, :], bin_row_h, "BINROW"),
        (EYE[:, :], eye_h, "EYE"),
        (EYEF[:, :], eye_h, "EYEF"),
    ]
    tot = 16 * len(setup_loads)
    for dst, src, buf in setup_loads:
        S.emit("gpsimd", (lambda d, s: (lambda e: e.dma_start(d, s)))(dst, src),
               reads=(), writes=(buf,), sem="DSET", inc=16, force_token_val=tot)

    S.emit("scalar", lambda e: e.activation(AVEC[:, :], CONSTS[:, 0:8], AF.Sigmoid),
           reads=("CONSTS",), writes=("AVEC",))
    S.emit("vector", lambda e: e.tensor_mul(BBIN[:, :], CONSTS[:, 8:16], CONSTS[:, 16:24]),
           reads=("CONSTS",), writes=("BBIN",))
    S.emit("vector", lambda e: e.memset(ABC[:, :], 1.0), writes=("ABC",))
    for ec in range(NDC):
        S.emit("vector",
               (lambda ec: (lambda e: e.tensor_scalar(
                   ABC[:, ec * TS:(ec + 1) * TS], ABC[:, ec * TS:(ec + 1) * TS],
                   AVEC[:, ec:ec + 1], None, ALU.mult)))(ec),
               reads=("ABC", "AVEC"), writes=("ABC",))
    S.emit("vector", lambda e: e.memset(M2Z[:, :], 0.0), writes=("M2Z",))
    S.emit("vector", lambda e: e.tensor_mul(BBINROW[:, :], BROW[:, :], BINROW[:, :]),
           reads=("BROW", "BINROW"), writes=("BBINROW",))
    for eh in range(2):
        S.emit("tensor",
               (lambda eh: (lambda e: e.matmul(
                   POUT[0][:, :], ONES[0:1, 0:128], BROW[0:1, eh * 512:(eh + 1) * 512],
                   start=True, stop=True)))(eh),
               reads=("ONES", "BROW"), writes=("POUT0",))
        S.emit("vector",
               (lambda eh: (lambda e: e.tensor_copy(
                   BB[:, eh * 512:(eh + 1) * 512], POUT[0][:, :])))(eh),
               reads=("POUT0",), writes=("BB",))
    for dc in range(NDC):
        S.emit("vector",
               (lambda dc: (lambda e: e.tensor_mul(
                   W_INS[:, dc * D:(dc + 1) * D], WSTR[0][:, dc * D:(dc + 1) * D],
                   BB[:, :])))(dc),
               reads=("WSTR0", "BB"), writes=("W_INS",))

    for h in range(4):
        S.emit("gpsimd",
               (lambda h: (lambda e: e.dma_start(
                   STG_M[:, :], maker_h[:, h * 2048:(h + 1) * 2048])))(h),
               reads=(), writes=("STG_M",), sem="DSTG_M", inc=16)
        S.emit("gpsimd",
               (lambda h: (lambda e: e.dma_start(
                   STG_U[:, :], up_h[:, h * 4096:(h + 1) * 4096])))(h),
               reads=(), writes=("STG_U",), sem="DSTG_U", inc=16)
        for r in range(64):
            pi = r % 2
            S.emit("tensor",
                   (lambda h, r, pi: (lambda e: e.matmul(
                       PMM[pi][0:32, 0:64],
                       STG_M[0:16, r * 32:(r + 1) * 32],
                       STG_U[0:16, r * 64:(r + 1) * 64],
                       start=True, stop=True)))(h, r, pi),
                   reads=("STG_M", "STG_U"), writes=(f"PMM{pi}",))
            g = r % 8
            g2, sub = g // 2, g % 2
            half, hp = h // 2, h % 2
            slot = ((r // 8) * 2 + sub) * 2 + half
            rr = g2 * 32
            cc0 = slot * 128 + hp * 64
            S.emit("vector",
                   (lambda pi, rr, cc0: (lambda e: e.tensor_copy(
                       M2Z[rr:rr + 32, cc0:cc0 + 64], PMM[pi][0:32, 0:64])))(pi, rr, cc0),
                   reads=(f"PMM{pi}",), writes=("M2Z",))

    def conv_group(g):
        sl = g % 2
        S.emit("gpsimd",
               (lambda g, sl: (lambda e: e.dma_start(
                   WSTR[sl][:, :],
                   w_arr[g * G:(g + 1) * G].rearrange("c f e -> f c e"))))(g, sl),
               reads=(), writes=(f"WSTR{sl}",), sem=f"DCL{sl}", inc=16)
        S.emit("sync",
               (lambda g, sl: (lambda e: e.dma_start(
                   w_bf[g * G:(g + 1) * G].rearrange("c f e -> f c e"),
                   WSTR[sl][:, :])))(g, sl),
               reads=(f"WSTR{sl}",), writes=(f"WBF{g}",), sem=f"DCS{sl}", inc=16)

    def x_load(st):
        sl = st % 4
        S.emit("gpsimd",
               (lambda st, sl: (lambda e: e.dma_start(
                   XRES[sl][:, :],
                   x_in[st * TS:(st + 1) * TS, :].rearrange("(a p) e -> p a e", p=128))))(st, sl),
               reads=(), writes=(f"XRES{sl}",), sem=f"DX{sl}", inc=16)

    def xbuf_k(sl):
        return f"XRES{sl}"

    def ln_stats(xbuf, tb, XH, hkey, apply_eng="vector"):
        o = tb * D
        S.emit("vector",
               (lambda o, xbuf: (lambda e: e.bn_stats(STATS[:, 0:6], XRES[xbuf][:, o:o + 512])))(o, xbuf),
               reads=(xbuf_k(xbuf),), writes=("STATS",))
        S.emit("vector",
               (lambda o, xbuf: (lambda e: e.bn_stats(STATS[:, 6:12], XRES[xbuf][:, o + 512:o + 1024])))(o, xbuf),
               reads=(xbuf_k(xbuf),), writes=("STATS",))
        ln_stats_tail(xbuf, tb, XH, hkey, apply_eng)

    def ln_stats_tail(xbuf, tb, XH, hkey, apply_eng="vector"):
        o = tb * D
        S.emit("vector", lambda e: e.bn_aggr(STATS[:, 12:14], STATS[:, 0:12]),
               reads=("STATS",), writes=("STATS",))
        S.emit("scalar", lambda e: e.activation(
            STATS[:, 15:16], STATS[:, 13:14], AF.Ln, bias=CONSTS[:, 24:25]),
            reads=("STATS", "CONSTS"), writes=("STATS",))
        S.emit("scalar", lambda e: e.activation(
            STATS[:, 14:15], STATS[:, 15:16], AF.Exp, scale=-0.5),
            reads=("STATS",), writes=("STATS",))
        if apply_eng == "vector":
            S.emit("vector",
                   (lambda o, tb, xbuf, XH: (lambda e: e.tensor_scalar(
                       XH[tb][:, :], XRES[xbuf][:, o:o + D],
                       STATS[:, 12:13], STATS[:, 14:15], ALU.subtract, ALU.mult)))(o, tb, xbuf, XH),
                   reads=(xbuf_k(xbuf), "STATS"), writes=(f"{hkey}{tb}",))
        else:
            S.emit("scalar", lambda e: e.mul(STATS[:, 15:16], STATS[:, 12:13],
                                             STATS[:, 14:15]),
                   reads=("STATS",), writes=("STATS",))
            S.emit("scalar", lambda e: e.mul(STATS[:, 15:16], STATS[:, 15:16], -1.0),
                   reads=("STATS",), writes=("STATS",))
            S.emit("scalar",
                   (lambda o, tb, xbuf, XH: (lambda e: e.activation(
                       XH[tb][:, :], XRES[xbuf][:, o:o + D], AF.Identity,
                       bias=STATS[:, 15:16], scale=STATS[:, 14:15])))(o, tb, xbuf, XH),
                   reads=(xbuf_k(xbuf), "STATS"), writes=(f"{hkey}{tb}",))

    def h_group(ec):
        po = ec % 4
        S.emit("tensor",
               (lambda ec, po: (lambda e: e.matmul(
                   PMM[po][:, 0:TS], BBINROW[0:1, ec * 128:(ec + 1) * 128],
                   ONES[0:1, 0:256], start=True, stop=False)))(ec, po),
               reads=("BBINROW", "ONES"), writes=(f"PMM{po}",))
        for dc in range(NDC):
            S.emit("tensor",
                   (lambda ec, dc, po: (lambda e: e.matmul(
                       PMM[po][:, 0:TS],
                       W_INS[:, dc * D + ec * 128: dc * D + (ec + 1) * 128],
                       XTA[:, dc * TS:(dc + 1) * TS],
                       start=False, stop=(dc == NDC - 1))))(ec, dc, po),
                   reads=("W_INS", f"XTAd{dc}_0", f"XTAd{dc}_1"), writes=(f"PMM{po}",))

    def attn_a(st, seq_pos, nst):
        S.emit("scalar", lambda e: e.activation(DUM[:, :], CONSTS[:, 24:25], AF.Ln),
               reads=("CONSTS",), writes=("DUM",))
        h_group(0)
        for ec in range(NDC):
            po = ec % 4
            if ec + 1 < NDC:
                h_group(ec + 1)
            init = 0.0 if seq_pos == 0 else STATE[:, ec:ec + 1]
            rd = (f"PMM{po}", "ABC") if seq_pos == 0 else (f"PMM{po}", "ABC", f"STATE{ec}")
            S.emit("vector",
                   (lambda ec, po, init: (lambda e: e.tensor_tensor_scan(
                       YBF[:, ec * TS:(ec + 1) * TS], ABC[:, ec * TS:(ec + 1) * TS],
                       PMM[po][:, 0:TS], init, ALU.mult, ALU.add)))(ec, po, init),
                   reads=rd, writes=(f"YBF{ec}",))
        if seq_pos != per_seq - 1:
            ybf3 = YBF[:, :].rearrange("p (n t) -> p n t", t=TS)
            S.emit("vector",
                   (lambda ybf3: (lambda e: e.tensor_copy(
                       STATE[:, 0:NDC], ybf3[:, :, TS - 1:TS])))(ybf3),
                   reads=tuple(f"YBF{ec}" for ec in range(NDC)),
                   writes=tuple(f"STATE{ec}" for ec in range(NDC)))

    def attn_b(xbuf, tr_next):
        for tb in range(2):
            for eh in range(2):
                S.emit("tensor",
                       (lambda tb, eh: (lambda e: e.matmul(
                           POUT[2 * tb + eh][:, :], ONES[0:1, 0:128],
                           BOUT[0:1, eh * 512:(eh + 1) * 512],
                           start=True, stop=False)))(tb, eh),
                       reads=("ONES", "BOUT"), writes=(f"POUT{2 * tb + eh}",))
            for eh in range(2):
                for ec in range(NDC):
                    S.emit("tensor",
                           (lambda ec, tb, eh: (lambda e: e.matmul(
                               POUT[2 * tb + eh][:, :],
                               YBF[:, ec * TS + tb * TB: ec * TS + (tb + 1) * TB],
                               W_OUT[:, ec * D + eh * 512: ec * D + (eh + 1) * 512],
                               start=False, stop=(ec == NDC - 1))))(ec, tb, eh),
                           reads=(f"YBF{ec}", "W_OUT"), writes=(f"POUT{2 * tb + eh}",))
            for eh in range(2):
                S.emit("vector",
                       (lambda tb, eh, xbuf: (lambda e: e.tensor_add(
                           XRES[xbuf][:, tb * D + eh * 512: tb * D + (eh + 1) * 512],
                           XRES[xbuf][:, tb * D + eh * 512: tb * D + (eh + 1) * 512],
                           POUT[2 * tb + eh][:, :])))(tb, eh, xbuf),
                       reads=(xbuf_k(xbuf), f"POUT{2 * tb + eh}"), writes=(xbuf_k(xbuf),))
                S.emit("vector",
                       (lambda tb, eh, xbuf: (lambda e: e.bn_stats(
                           STATS[:, 6 * eh:6 * eh + 6],
                           XRES[xbuf][:, tb * D + eh * 512: tb * D + (eh + 1) * 512])))(tb, eh, xbuf),
                       reads=(xbuf_k(xbuf),), writes=("STATS",))
            ln_stats_tail(xbuf, tb, XHF, "XHF", "scalar")
        gelu_preload()
        pe_trF()
        if tr_next:
            dma_tr_next()

    def ln1_for(s2):
        ln_stats(s2 % 4, 0, XHA, "XHA")
        ln_stats(s2 % 4, 1, XHA, "XHA")

    def gelu_preload():
        S.emit("scalar", lambda e: e.activation(DUM[:, :], CONSTS[:, 24:25], AF.Gelu),
               reads=("CONSTS",), writes=("DUM",))

    def dma_tr_next():
        tot = S.semv.get("DTR", 0) + 16 * 2 * NDC
        for tb in range(2):
            for dc in range(NDC):
                S.emit("sync",
                       (lambda tb, dc: (lambda e: e.dma_start_transpose(
                           XTA[:, dc * TS + tb * TB: dc * TS + (tb + 1) * TB],
                           XHA[tb][:, dc * 128:(dc + 1) * 128])))(tb, dc),
                       reads=(f"XHA{tb}",), writes=(f"XTAd{dc}_{tb}",),
                       sem="DTR", inc=16, force_token_val=tot)

    def pe_trF():
        for k in range(2 * NDC):
            tb, dc = k // NDC, k % NDC
            j, sl = k % 4, (k // 4) % 2
            dst = PMM[j][:, TS + sl * 128: TS + (sl + 1) * 128]
            S.emit("tensor",
                   (lambda tb, dc, dst: (lambda e: e.transpose(
                       dst, XHF[tb][:, dc * 128:(dc + 1) * 128], EYEF[:, :])))(tb, dc, dst),
                   reads=(f"XHF{tb}", "EYEF"), writes=(f"PMM{j}",))
            S.emit("scalar",
                   (lambda tb, dc, dst: (lambda e: e.copy(
                       XTF[:, dc * TS + tb * TB: dc * TS + (tb + 1) * TB], dst)))(tb, dc, dst),
                   reads=(f"PMM{j}",), writes=(f"XTFd{dc}_{tb}",))

    def w_group_load(st, g):
        sl = g % 3
        S.emit("sync",
               (lambda g, sl: (lambda e: e.dma_start(
                   WSTR[sl][:, :],
                   w_bf[g * G:(g + 1) * G].rearrange("c f e -> f c e"))))(g, sl),
               reads=(f"WBF{g}",), writes=(f"WSTR{sl}",), sem=f"DW{sl}", inc=16)

    def ffn(st, xbuf):
        nsl = 2 if st == 0 else 3
        pref = 2 if st == 0 else 3

        def cproj(i):
            c = FFN_ORDER[i]
            g, cc, po = i // G, i % G, i % 8
            sl = g % nsl
            for tb in range(2):
                for eh in range(2):
                    S.emit("tensor",
                           (lambda i, cc, sl, po, tb, eh: (lambda e: e.matmul(
                               POUT[2 * tb + eh][:, :],
                               HC[po][:, tb * TB:(tb + 1) * TB],
                               WSTR[sl][:, cc * D + eh * 512: cc * D + (eh + 1) * 512],
                               start=(i == 0), stop=(i == NCH - 1))))(i, cc, sl, po, tb, eh),
                           reads=(f"HC{po}", f"WSTR{sl}"), writes=(f"POUT{2 * tb + eh}",))
            if cc == G - 1 and g + pref < NG:
                if st == 0:
                    conv_group(g + pref)
                else:
                    w_group_load(st, g + pref)

        def zmm(i):
            c = FFN_ORDER[i]
            r = c // 2
            g2 = (r % 8) // 2
            slot = ((r // 8) * 2 + ((r % 8) % 2)) * 2 + (c % 2)
            po = i % 4
            pap = PMM[po][:, 0:TS]
            pkey = f"PMM{po}"
            S.emit("tensor",
                   (lambda c, g2, slot, pap: (lambda e: e.matmul(
                       pap,
                       M2Z[g2 * 32:(g2 + 1) * 32, slot * 128:(slot + 1) * 128],
                       XTF[g2 * 32:(g2 + 1) * 32, (c // 16) * TS:((c // 16) + 1) * TS],
                       start=True, stop=True,
                       tile_position=(g2 * 32, 0))))(c, g2, slot, pap),
                   reads=("M2Z", f"XTFd{c // 16}_0", f"XTFd{c // 16}_1"),
                   writes=(pkey,))
            S.emit("scalar",
                   (lambda pap, hpo: (lambda e: e.activation(
                       HC[hpo][:, :], pap, AF.Gelu)))(pap, i % 8),
                   reads=(pkey,), writes=(f"HC{i % 8}",))

        done = 0
        for k in range(NCH // 4):
            for j in range(4):
                zmm(4 * k + j)
            while done < 4 * k:
                cproj(done)
                done += 1
        while done < NCH:
            cproj(done)
            done += 1

    def ffn_tail(st, xbuf):
        for tb in range(2):
            for eh in range(2):
                S.emit("vector",
                       (lambda tb, eh, xbuf: (lambda e: e.tensor_add(
                           XRES[xbuf][:, tb * D + eh * 512: tb * D + (eh + 1) * 512],
                           XRES[xbuf][:, tb * D + eh * 512: tb * D + (eh + 1) * 512],
                           POUT[2 * tb + eh][:, :])))(tb, eh, xbuf),
                       reads=(xbuf_k(xbuf), f"POUT{2 * tb + eh}"), writes=(xbuf_k(xbuf),))

    def x_store(st):
        sl = st % 4
        S.emit("gpsimd",
               (lambda st, sl: (lambda e: e.dma_start(
                   y_out[st * TS:(st + 1) * TS, :].rearrange("(a p) e -> p a e", p=128),
                   XRES[sl][:, :])))(st, sl),
               reads=(f"XRES{sl}",), writes=(), sem=f"DXO{sl}", inc=16)

    x_load(0)
    x_load(1)
    x_load(2)
    ln1_for(0)
    conv_group(0)
    conv_group(1)
    for st in range(nst):
        xbuf = st % 4
        seq_pos = st % per_seq
        if st > 0:
            w_group_load(st, 0)
            w_group_load(st, 1)
            w_group_load(st, 2)
        if st == 0:
            dma_tr_next()
            if nst > 1:
                ln1_for(1)
        attn_a(st, seq_pos, nst)
        attn_b(xbuf, st + 1 < nst)
        ffn(st, xbuf)
        if st + 3 < nst:
            x_load(st + 3)
        if st + 2 < nst:
            ln1_for(st + 2)
        ffn_tail(st, xbuf)
        x_store(st)

    for _s in ("DXO0", "DXO1", "DXO2", "DXO3"):
        S.raw_wait("gpsimd", _s, S.semv.get(_s, 0))

    def replay(name):
        def f(eng):
            for waits, fn, sem, inc in S.ops[name]:
                for s, v in waits:
                    eng.wait_ge(sems[s], v)
                if fn is not None:
                    fn(eng).then_inc(sems[sem], inc)
        return f

    with nc.Block() as block:
        block.sync(replay("sync"))
        block.gpsimd(replay("gpsimd"))
        block.tensor(replay("tensor"))
        block.vector(replay("vector"))
        block.scalar(replay("scalar"))

    ctx.close()
    return nc


def host_stage(L0_attn_w_in, L0_attn_b_in, L0_attn_a, L0_attn_b, L0_attn_w_out,
               L0_attn_b_out, L0_ffn_rail_maker, L0_ffn_rail_up_proj, L0_ffn_w_cproj):
    f32 = np.float32
    w_inT = np.ascontiguousarray(np.asarray(L0_attn_w_in, f32).T).reshape(NDC, 128, D)
    w_outT = np.ascontiguousarray(np.asarray(L0_attn_w_out, f32).T).reshape(NDC, 128, D)
    consts = np.zeros((128, 32), f32)
    consts[:, 0:8] = np.asarray(L0_attn_a, f32).reshape(8, 128).T
    consts[:, 8:16] = np.asarray(L0_attn_b, f32).reshape(8, 128).T
    consts[:, 16:24] = np.asarray(L0_attn_b_in, f32).reshape(8, 128).T
    consts[:, 24] = 1e-5
    b_out_h = np.ascontiguousarray(np.asarray(L0_attn_b_out, f32)[None, :])
    ones_h = np.ones((1, 256), f32)
    b_row_h = np.ascontiguousarray(np.asarray(L0_attn_b, f32)[None, :])
    bin_row_h = np.ascontiguousarray(np.asarray(L0_attn_b_in, f32)[None, :])
    eye_h = np.eye(128, dtype=f32)
    mk = np.asarray(L0_ffn_rail_maker, f32)
    maker_h = np.zeros((16, 8192), f32)
    for h in range(4):
        for r in range(64):
            sub = (r % 8) % 2
            c0 = (h * 64 + r) * 32 + sub * 16
            maker_h[:, c0:c0 + 16] = mk[h, r].T
    maker_h = np.ascontiguousarray(maker_h)
    up_h = np.ascontiguousarray(
        np.asarray(L0_ffn_rail_up_proj, f32).transpose(3, 0, 1, 2).reshape(16, FH))
    wt = np.ascontiguousarray(np.asarray(L0_ffn_w_cproj, f32).T)
    wt = wt.reshape(2, 2, 64, 64, D)
    wt = wt.transpose(2, 0, 1, 3, 4)
    w_arr = np.ascontiguousarray(wt.reshape(NCH, 128, D)[FFN_ORDER])
    return dict(w_inT=w_inT, w_outT=w_outT, consts=consts, b_out_h=b_out_h,
                ones_h=ones_h, b_row_h=b_row_h, bin_row_h=bin_row_h,
                eye_h=eye_h, maker_h=maker_h, up_h=up_h, w_arr=w_arr)


def kernel(L0_attn_x, L0_attn_w_in, L0_attn_b_in, L0_attn_a, L0_attn_b,
           L0_attn_w_out, L0_attn_b_out, L0_ffn_rail_maker, L0_ffn_rail_up_proj,
           L0_ffn_w_cproj):
    x = np.ascontiguousarray(np.asarray(L0_attn_x, np.float32))
    B, T, _ = x.shape
    nseq = B // NCORES
    weights = host_stage(L0_attn_w_in, L0_attn_b_in, L0_attn_a, L0_attn_b,
                         L0_attn_w_out, L0_attn_b_out, L0_ffn_rail_maker,
                         L0_ffn_rail_up_proj, L0_ffn_w_cproj)
    nc = build_nc(nseq, T)
    in_maps = []
    for core in range(NCORES):
        m = dict(weights)
        m["x_in"] = np.ascontiguousarray(
            x[core * nseq:(core + 1) * nseq].reshape(nseq * T, D))
        in_maps.append(m)
    res = run_bass_kernel_spmd(nc, in_maps, core_ids=list(range(NCORES)))
    outs = [np.asarray(r["y_out"]).reshape(nseq, T, D) for r in res.results]
    return np.concatenate(outs, axis=0)
